# version 08
# baseline (speedup 1.0000x reference)
import numpy as np
import ml_dtypes
from contextlib import ExitStack

import concourse.bass as bass
import concourse.mybir as mybir
from concourse.bass_utils import run_bass_kernel_spmd

F32 = mybir.dt.float32
BF16 = mybir.dt.bfloat16
AF = mybir.ActivationFunctionType
ALU = mybir.AluOpType
AX = mybir.AxisListType

NC_ = 8
SEQ = 2
S = 2048
D = 1024
W = 512
NWIN = 4
GS = 32
F = 4096
NE = 8
NXP = 10
TB = 1024
NBLK = (SEQ * S) // TB
SCALE = 0.125
EPS = 1e-5
BG = float(S - W)

o_wq, o_wk, o_wv = 0, 8192, 16384
o_attnout = 24576
o_q, o_k = 40960, 45056
o_v = 49152
o_p = 53248
o_lnwfm = 54272
o_lnwtm = 58368
o_lngfm = 62464
o_qg, o_kg = 62976, 63488
o_vg = 64000
o_dbf = 65024
o_stage = 66048
AB_SZ = 70144
b_h = 0
b_wp = 16384
b_ln2fm = 32768
b_wfc = 40960
b_ln2tm = 49152

a_xsub = 0
a_y = 2048
a_xt = 4096
a_junk = 6144
a_dn = 7168
a_xg = 9216
a_gs = 10240
a_cols = 10560
a_vw = 10656
a_cent = 10752
b_acc = 0
b_x2 = 8192
b_f32st = 12288
b_cols = 12800
b_call = 12896
b_raw = 12960
b_gall = 13024
b_tmk = 13088
b_rt = 13152
AF_SZ = 13600

ENGS = ("sync", "gpsimd", "vector", "scalar", "tensor")
QUEUES = ("sync", "gpsimd")
SEM_OF = {"sync": "qS", "gpsimd": "qG", "vector": "tVEC", "scalar": "tACT",
          "tensor": "tMM"}
GATE_EXCLUDE_PFX = ("sfc.", "spj.", "s0f.", "s0p.", "s0a.", "stgF", "stgP",
                    "x2d.", "outst.")
GATE_EXCLUDE = {"identb", "identf", "selrow", "selbc", "wmask16", "onesb",
                "onesf", "ones128", "biasr", "centT", "rbrow", "rbbc8", "cent",
                "cpool"}


class Prog:
    NSLOT = 12

    def __init__(self):
        self.ops = []
        self.res = {}

    def op(self, eng, build, reads=(), writes=(), dma=False, wsmall=False):
        i = len(self.ops)
        deps = set()
        for r in reads:
            st = self.res.get(r)
            if st and st["w"] is not None:
                deps.add(st["w"])
        for r in writes:
            st = self.res.get(r)
            if st:
                if st["w"] is not None:
                    deps.add(st["w"])
                deps.update(st["r"])
        keep = set()
        for j in deps:
            if (self.ops[j][0] == eng and eng not in QUEUES
                    and not self.ops[j][4]
                    and not (eng == "vector" and self.ops[j][5])):
                continue
            keep.add(j)
        for r in reads:
            self.res.setdefault(r, {"w": None, "r": []})["r"].append(i)
        for r in writes:
            self.res[r] = {"w": i, "r": []}
        self.ops.append([eng, build, keep, writes, dma, wsmall])
        return i

    def dma(self, eng, out, in_, reads=(), writes=(), **kw):
        return self.op(eng, lambda e: e.dma_start(out=out, in_=in_, **kw),
                       reads, writes, dma=True)

    def dmat(self, eng, out, in_, reads=(), writes=()):
        return self.op(eng, lambda e: e.dma_start_transpose(out=out, in_=in_),
                       reads, writes, dma=True)

    def gate(self):
        regs = [r for r in self.res
                if r not in GATE_EXCLUDE
                and not any(r.startswith(p) for p in GATE_EXCLUDE_PFX)]
        return self.op("vector", lambda e: e.engine_nop(),
                       reads=(), writes=tuple(regs))

    def finalize(self, nc, block, stk):
        K = Prog.NSLOT
        names = ["tMM", "tACT", "tVEC"]
        names += [f"qS{r}" for r in range(K)] + [f"qG{r}" for r in range(K)]
        sems = {s: stk.enter_context(nc.semaphore(s)) for s in names}
        vpos = {}
        vp = 0
        for i, o in enumerate(self.ops):
            if o[0] == "vector":
                vpos[i] = vp
                vp += 1
        pruned = [set() for _ in self.ops]
        needed = set()
        for i, o in enumerate(self.ops):
            for j in o[2]:
                if (o[0] == "vector" and self.ops[j][0] == "vector"
                        and not self.ops[j][4] and self.ops[j][5]
                        and vpos[i] - vpos[j] >= 8):
                    pruned[i].add(j)
                    continue
                needed.add(j)
        ccnt = {"tMM": 0, "tACT": 0, "tVEC": 0}
        qocc = {"sync": 0, "gpsimd": 0}
        val = [None] * len(self.ops)
        incs = [None] * len(self.ops)
        selfw = [None] * len(self.ops)
        for i, (eng, _, _, _, is_dma, _) in enumerate(self.ops):
            if is_dma:
                qpfx = "qS" if eng == "sync" else "qG"
                occ = qocc[eng]
                qocc[eng] = occ + 1
                slot = f"{qpfx}{occ % K}"
                rnd = occ // K
                if rnd > 0:
                    selfw[i] = (slot, 16 * rnd)
                val[i] = (slot, 16 * (rnd + 1))
                incs[i] = (slot, 16)
            elif i in needed:
                sm = SEM_OF[eng]
                ccnt[sm] += 1
                val[i] = (sm, ccnt[sm])
                incs[i] = (sm, 1)
        streams = {e: [] for e in ENGS}
        for i, (eng, build, deps, _, _, _) in enumerate(self.ops):
            waits = {}
            for j in deps:
                if j in pruned[i] or val[j] is None:
                    continue
                sj, vj = val[j]
                waits[sj] = max(waits.get(sj, 0), vj)
            if selfw[i] is not None:
                sj, vj = selfw[i]
                waits[sj] = max(waits.get(sj, 0), vj)
            streams[eng].append((waits, build, incs[i]))
        for eng in ENGS:
            items = streams[eng]

            def body(e, items=items):
                seen = {}
                for waits, build, inc in items:
                    for s_, v_ in sorted(waits.items()):
                        if seen.get(s_, 0) >= v_:
                            continue
                        e.wait_ge(sems[s_], v_)
                        seen[s_] = v_
                    ins = build(e)
                    if inc is not None:
                        ins.then_inc(sems[inc[0]], inc[1])
            getattr(block, eng)(body)


def build(dbg_x2=False, only_a=False):
    nc = bass.Bass("TRN2", target_bir_lowering=False,
                   detect_race_conditions=False)

    x_in = nc.dram_tensor("x", [SEQ, S, D], F32, kind="ExternalInput").ap()
    wq = nc.dram_tensor("wq", [D, D], F32, kind="ExternalInput").ap()
    wk = nc.dram_tensor("wk", [D, D], F32, kind="ExternalInput").ap()
    wv = nc.dram_tensor("wv", [D, D], F32, kind="ExternalInput").ap()
    wo = nc.dram_tensor("wo", [D, D], F32, kind="ExternalInput").ap()
    bq = nc.dram_tensor("bq", [1, D], F32, kind="ExternalInput").ap()
    bk = nc.dram_tensor("bk", [1, D], F32, kind="ExternalInput").ap()
    bv = nc.dram_tensor("bv", [1, D], F32, kind="ExternalInput").ap()
    bo = nc.dram_tensor("bo", [1, D], F32, kind="ExternalInput").ap()
    cent = nc.dram_tensor("cent", [NE, D], F32, kind="ExternalInput").ap()
    rbias = nc.dram_tensor("rbias", [1, NE], F32, kind="ExternalInput").ap()
    wfc_r = nc.dram_tensor("wfc_r", [NE, F, D], F32, kind="ExternalInput").ap()
    wpj_r = nc.dram_tensor("wpj_r", [NE, D, F], F32, kind="ExternalInput").ap()
    wfc_s = nc.dram_tensor("wfc_s", [2, F, D], F32, kind="ExternalInput").ap()
    wpj_s = nc.dram_tensor("wpj_s", [2, D, F], F32, kind="ExternalInput").ap()
    c_identb = nc.dram_tensor("c_identb", [128, 128], BF16, kind="ExternalInput").ap()
    c_identf = nc.dram_tensor("c_identf", [128, 128], F32, kind="ExternalInput").ap()
    c_selrow = nc.dram_tensor("c_selrow", [128, 256], BF16, kind="ExternalInput").ap()
    c_selbc = nc.dram_tensor("c_selbc", [16, 1024], F32, kind="ExternalInput").ap()
    c_wmask16 = nc.dram_tensor("c_wmask16", [64, 1024], BF16, kind="ExternalInput").ap()

    out_d = nc.dram_tensor("out", [SEQ, S, D], F32, kind="ExternalOutput").ap()
    out_fl = out_d.rearrange("s t d -> (s t) d")
    x2k = "ExternalOutput" if dbg_x2 else "Internal"
    x2d = nc.dram_tensor("x2d", [SEQ * S, D], F32, kind=x2k).ap()
    s0a = nc.dram_tensor("s0a", [4, D, D], BF16, kind="Internal").ap()
    s0f = nc.dram_tensor("s0f", [NXP, F, D], BF16, kind="Internal").ap()
    s0p = nc.dram_tensor("s0p", [NXP, D, F], BF16, kind="Internal").ap()
    sfc = nc.dram_tensor("sfc", [NXP, 8, 128, F], BF16, kind="Internal").ap()
    spj = nc.dram_tensor("spj", [NXP, 32, 128, D], BF16, kind="Internal").ap()

    fc_srcs = [wfc_s[0], wfc_s[1]] + [wfc_r[i] for i in range(NE)]
    pj_srcs = [wpj_s[0], wpj_s[1]] + [wpj_r[i] for i in range(NE)]
    awsrc = {0: wq, 1: wk, 2: wv, 3: wo}

    with ExitStack() as stk:
        e_ = stk.enter_context
        AB = e_(nc.sbuf_tensor("AB", [128, AB_SZ], BF16))
        AFT = e_(nc.sbuf_tensor("AFT", [128, AF_SZ], F32))
        identb = e_(nc.sbuf_tensor("identb", [128, 128], BF16))
        identf = e_(nc.sbuf_tensor("identf", [128, 128], F32))
        selrow = e_(nc.sbuf_tensor("selrow", [128, 256], BF16))
        selbc = e_(nc.sbuf_tensor("selbc", [16, 1024], F32))
        wmask16 = e_(nc.sbuf_tensor("wmask16", [64, 1024], BF16))
        onesb = e_(nc.sbuf_tensor("onesb", [1, 512], BF16))
        onesf = e_(nc.sbuf_tensor("onesf", [1, 128], F32))
        ones128 = e_(nc.sbuf_tensor("ones128", [128, 16], BF16))
        biasr = e_(nc.sbuf_tensor("biasr", [1, 4 * D], BF16))
        centT = e_(nc.sbuf_tensor("centT", [128, 64], F32))
        rbrow = e_(nc.sbuf_tensor("rbrow", [1, 8], F32))
        rbbc8 = e_(nc.sbuf_tensor("rbbc8", [128, 64], F32))
        cpool = e_(nc.sbuf_tensor("cpool", [128, 2], F32))

        ps_f0 = e_(nc.psum_tensor("ps_f0", [128, 1024], F32))
        ps_f1 = e_(nc.psum_tensor("ps_f1", [128, 1024], F32))
        ps_w = e_(nc.psum_tensor("ps_w", [128, 1024], F32))
        ps_t1 = e_(nc.psum_tensor("ps_t1", [128, 1024], BF16))
        ps_t2 = e_(nc.psum_tensor("ps_t2", [128, 1024], BF16))
        ps_a, ps_b = ps_f0[:, 0:512], ps_f0[:, 512:1024]
        ps_c, ps_d = ps_f1[:, 0:512], ps_f1[:, 512:1024]
        ps_w0, ps_w1 = ps_w[:, 0:512], ps_w[:, 512:1024]
        ps_tt = [ps_t1, ps_t2]
        R0, R1, R2, R3 = "ps0", "ps1", "ps2", "ps3"
        RW0, RW1 = "psW0", "psW1"
        RW = (RW0, RW1)
        RT1, RT2 = "psT1", "psT2"
        tbank = None

        P = Prog()
        block = e_(nc.Block())

        def mm(ps_ap, l, r, start, stop, reads=(), writes=()):
            return P.op("tensor",
                        lambda e, o=ps_ap, a=l, b=r, st=start, sp=stop:
                        e.matmul(o, a, b, start=st, stop=sp),
                        reads, writes)

        def transp(ps_ap, in_ap, ident, reads=(), writes=()):
            return P.op("tensor",
                        lambda e, o=ps_ap, i=in_ap, d=ident: e.transpose(o, i, d),
                        reads, writes)

        def act(out_ap, in_ap, fn, bias=0.0, scale=1.0, accum=None,
                reads=(), writes=()):
            return P.op("scalar",
                        lambda e, o=out_ap, i=in_ap, f=fn, b=bias, s=scale, a=accum:
                        e.activation(o, i, f, bias=b, scale=s, accum_out=a),
                        reads, writes)

        def vop(name, *args, reads=(), writes=()):
            small = False
            try:
                shp = args[0].shape
                free = 1
                for d in shp[1:]:
                    free *= d
                small = free <= 64
            except Exception:
                pass
            return P.op("vector",
                        lambda e, n=name, a=args: getattr(e, n)(*a),
                        reads, writes, wsmall=small)

        P.dma("sync", identb[:], c_identb, writes=("identb",))
        P.dma("sync", identf[:], c_identf, writes=("identf",))
        P.dma("sync", selrow[:], c_selrow, writes=("selrow",))
        P.dma("sync", selbc[:], c_selbc, writes=("selbc",))
        P.dma("sync", wmask16[:], c_wmask16, writes=("wmask16",))
        cn = AFT[0:8, a_cent:a_cent + D]
        P.dma("sync", cn, cent, writes=("cent",))
        P.dma("sync", rbrow[:], rbias, writes=("rbrow",))
        vop("memset", onesb[:], 1.0, writes=("onesb",))
        vop("memset", onesf[:], 1.0, writes=("onesf",))
        vop("memset", ones128[:], 1.0, writes=("ones128",))
        vop("memset", cpool[:, 0:1], EPS, writes=("cpool",))
        vop("memset", cpool[:, 1:2], BG, writes=("cpool",))
        P.dma("gpsimd", biasr[:1, 0:D], bq, writes=("biasr",))
        P.dma("gpsimd", biasr[:1, D:2 * D], bk, writes=("biasr",))
        P.dma("gpsimd", biasr[:1, 2 * D:3 * D], bv, writes=("biasr",))
        P.dma("gpsimd", biasr[:1, 3 * D:4 * D], bo, writes=("biasr",))
        for i in range(4):
            P.dma("gpsimd", s0a[i], awsrc[i], writes=(f"s0a.{i}",))

        def attn_wT(slot_off, mi, region):
            for ic in range(8):
                P.dmat("sync", AB[:, slot_off + ic * D: slot_off + (ic + 1) * D],
                       s0a[mi][:, ic * 128:(ic + 1) * 128],
                       reads=(f"s0a.{mi}",), writes=(region,))

        attn_wT(o_wq, 0, "wqslot")
        attn_wT(o_wk, 1, "wkT")
        attn_wT(o_wv, 2, "wvT")

        tbank = [(ps_a, R0), (ps_c, R2), (ps_w0, RW0), (ps_w1, RW1)]
        for dd in range(8):
            pb, rg = tbank[dd % 4]
            transp(pb[:, 0:8], cn[:, dd * 128:(dd + 1) * 128], identf[0:8, 0:8],
                   reads=("cent", "identf"), writes=(rg,))
            vop("tensor_copy", centT[:, dd * 8:(dd + 1) * 8], pb[:, 0:8],
                reads=(rg,), writes=("centT",))
        mm(ps_b[:, 0:8], onesf[:1, 0:128], rbrow[:1, 0:8], True, True,
           reads=("onesf", "rbrow"), writes=(R1,))
        for i in range(8):
            vop("tensor_copy", rbbc8[:, i * 8:(i + 1) * 8], ps_b[:, 0:8],
                reads=(R1,), writes=("rbbc8",))

        pre_all = []

        def mk_ffn_steps(m):
            def cast_fc(h):
                P.dma("gpsimd", s0f[m][h * 2048:(h + 1) * 2048, :],
                      fc_srcs[m][h * 2048:(h + 1) * 2048, :], writes=(f"s0f.{m}",))
            pre_all.append(lambda: cast_fc(0))
            pre_all.append(lambda: cast_fc(1))

            def xst_fc(dd, fh):
                stg = AB[:, o_stage: o_stage + 2048]
                P.dmat("sync", stg,
                       s0f[m][fh * 2048:(fh + 1) * 2048, dd * 128:(dd + 1) * 128],
                       reads=(f"s0f.{m}",), writes=("stgF",))
                P.dma("sync", sfc[m][dd][:, fh * 2048:(fh + 1) * 2048], stg,
                      reads=("stgF",), writes=(f"sfc.{m}",))
            for dd in range(8):
                pre_all.append(lambda dd=dd: (xst_fc(dd, 0), xst_fc(dd, 1)))

            def cast_pj(h):
                P.dma("gpsimd", s0p[m][h * 512:(h + 1) * 512, :],
                      pj_srcs[m][h * 512:(h + 1) * 512, :], writes=(f"s0p.{m}",))
            pre_all.append(lambda: cast_pj(0))
            pre_all.append(lambda: cast_pj(1))

            def xst_pj(j):
                stg = AB[:, o_stage + 2048 + (j % 2) * D:
                         o_stage + 2048 + (j % 2) * D + D]
                rg = f"stgP{j % 2}"
                P.dmat("sync", stg, s0p[m][:, j * 128:(j + 1) * 128],
                       reads=(f"s0p.{m}",), writes=(rg,))
                P.dma("sync", spj[m][j], stg, reads=(rg,), writes=(f"spj.{m}",))
            for j0 in range(0, 32, 4):
                pre_all.append(lambda j0=j0: [xst_pj(j) for j in range(j0, j0 + 4)])

        for m in range(NXP):
            mk_ffn_steps(m)

        def flush_pre(n):
            for _ in range(n):
                if pre_all:
                    pre_all.pop(0)()

        def ln_stats_batch(tiles, junks, colbuf, col_rgs, i0, n, want_nmr=False):
            G = len(tiles)
            for g, (tile, trg) in enumerate(tiles):
                vop("reduce_sum", colbuf[:, 16 + i0 + g:16 + i0 + g + 1], tile,
                    AX.X, reads=(trg,), writes=col_rgs)
            for g, (tile, trg) in enumerate(tiles):
                jk, jrg = junks[g % len(junks)]
                act(jk, tile, AF.Square,
                    accum=colbuf[:, 24 + i0 + g:24 + i0 + g + 1],
                    reads=(trg,), writes=(jrg,) + col_rgs)
            mean = colbuf[:, 32 + i0:32 + i0 + G]
            var = colbuf[:, 40 + i0:40 + i0 + G]
            rstd = colbuf[:, 0 + i0:0 + i0 + G]
            nmean = colbuf[:, 8 + i0:8 + i0 + G]
            ssum = colbuf[:, 16 + i0:16 + i0 + G]
            ssq = colbuf[:, 24 + i0:24 + i0 + G]
            vop("tensor_scalar_mul", mean, ssum, 1.0 / n,
                reads=col_rgs, writes=col_rgs)
            vop("tensor_tensor", var, mean, mean, ALU.mult,
                reads=col_rgs, writes=col_rgs)
            vop("scalar_tensor_tensor", var, ssq, 1.0 / n, var, ALU.mult,
                ALU.subtract, reads=col_rgs, writes=col_rgs)
            act(var, var, AF.Ln, bias=cpool[0:colbuf.shape[0], 0:1],
                reads=col_rgs + ("cpool",), writes=col_rgs)
            act(rstd, var, AF.Exp, scale=-0.5, reads=col_rgs, writes=col_rgs)
            vop("tensor_scalar_mul", nmean, mean, -1.0,
                reads=col_rgs, writes=col_rgs)
            if want_nmr:
                nmr = colbuf[:, 48 + i0:48 + i0 + G]
                vop("tensor_tensor", nmr, nmean, rstd, ALU.mult,
                    reads=col_rgs, writes=col_rgs)

        wqT = AB[:, o_wq:o_wq + 8192]
        wkT = AB[:, o_wk:o_wk + 8192]
        wvT = AB[:, o_wv:o_wv + 8192]

        def emit_a1(s, ctx):
            xg = AFT[0:64, a_xg:a_xg + D]
            P.dma("gpsimd", xg, x_in[s, 0:S:GS, :], writes=("xg",))
            gj = AFT[0:64, a_junk:a_junk + D]
            gc = AFT[0:64, a_cols:a_cols + 64]
            ln_stats_batch([(xg, "xg")], [(gj, "junk")], gc, ("cols.g",), 7, D)
            lngtm = AB[0:64, b_ln2tm:b_ln2tm + D]
            vop("tensor_scalar", lngtm, xg, gc[:, 8 + 7:8 + 8], gc[:, 0 + 7:0 + 8],
                ALU.add, ALU.mult, reads=("xg", "cols.g"), writes=("lngtm",))
            lngf = AB[:, o_lngfm:o_lngfm + 512]
            for dd in range(8):
                rt_ = (RT1, RT2)[dd % 2]
                transp(ps_tt[dd % 2][:, 0:64], lngtm[:, dd * 128:(dd + 1) * 128],
                       identb[0:64, 0:64], reads=("lngtm", "identb"), writes=(rt_,))
                vop("tensor_copy", lngf[:, dd * 64:(dd + 1) * 64],
                    ps_tt[dd % 2][:, 0:64], reads=(rt_,), writes=("lngf",))
            qg = AB[0:64, o_qg:o_qg + 1024]
            kg = AB[0:64, o_p:o_p + 1024]
            vg = AB[0:64, o_vg:o_vg + 1024]
            for dst, dst_rgs, wT, w_rg, bofs in (
                    (qg, ("qg",), wqT, "wqslot", 0),
                    (kg, ("p.0", "p.1"), wkT, "wkT", D)):
                for bank in range(2):
                    psl = (ps_w0, ps_w1)[bank][0:64, :]
                    for hh in range(8):
                        h = bank * 8 + hh
                        for ic in range(8):
                            mm(psl[:, hh * 64:(hh + 1) * 64],
                               wT[:, ic * D + h * 64: ic * D + h * 64 + 64],
                               lngf[:, ic * 64:(ic + 1) * 64],
                               (hh == 0 and ic == 0), False,
                               reads=(w_rg, "lngf"), writes=(RW[bank],))
                        mm(psl[:, hh * 64:(hh + 1) * 64],
                           biasr[:1, bofs + h * 64: bofs + h * 64 + 64],
                           onesb[:1, 0:64],
                           False, (hh == 7),
                           reads=("biasr", "onesb"), writes=(RW[bank],))
                act(dst, ps_w[0:64, :], AF.Identity, reads=RW, writes=dst_rgs)
            for half in range(2):
                psl = (ps_w0, ps_w1)[half][0:64, :]
                for ic in range(8):
                    mm(psl, lngf[:, ic * 64:(ic + 1) * 64],
                       wvT[:, ic * D + half * 512: ic * D + half * 512 + 512],
                       ic == 0, False, reads=("wvT", "lngf"), writes=(RW[half],))
                mm(psl, onesb[:1, 0:64],
                   biasr[:1, 2 * D + half * 512: 2 * D + half * 512 + 512],
                   False, True, reads=("biasr", "onesb"), writes=(RW[half],))
            act(vg, ps_w[0:64, :], AF.Identity, reads=RW, writes=("vg",))
            dbf = AB[0:64, o_dbf:o_dbf + 1024]
            bsg = AFT[0:64, a_xsub:a_xsub + 1024]
            bmg = AFT[0:64, a_y:a_y + 1024]
            for h in range(16):
                pb_ps, pb_rg = ((ps_f0, (R0, R1)), (ps_f1, (R2, R3)))[h // 8]
                hh = h % 8
                mm(pb_ps[0:64, hh * 64:(hh + 1) * 64],
                   kg[:, h * 64:(h + 1) * 64], qg[:, h * 64:(h + 1) * 64],
                   hh == 0, hh == 7,
                   reads=("p.0", "p.1", "qg"), writes=pb_rg)
            vop("tensor_scalar_mul", bsg[:, 0:512], ps_f0[0:64, 0:512], SCALE,
                reads=(R0, R1), writes=("xsub.0",))
            vop("tensor_scalar_mul", bsg[:, 512:1024], ps_f1[0:64, 0:512], SCALE,
                reads=(R2, R3), writes=("xsub.0",))
            vop("tensor_tensor", bmg, bsg, wmask16[:], ALU.mult,
                reads=("xsub.0", "wmask16"), writes=("y.0",))
            vop("tensor_tensor", bsg, bsg, bmg, ALU.add,
                reads=("xsub.0", "y.0"), writes=("xsub.0",))
            act(bsg, bsg, AF.Exp, reads=("xsub.0",), writes=("xsub.0",))
            act(bmg, bmg, AF.Exp, reads=("y.0",), writes=("y.0",))
            vop("tensor_tensor", dbf, bsg, bmg, ALU.subtract,
                reads=("xsub.0", "y.0"), writes=("dbf",))

        def emit_main(s, a1_hook=None):
            lnwtm = AB[:, o_lnwtm:o_lnwtm + 4096]
            lnwfm = AB[:, o_lnwfm:o_lnwfm + 4096]
            qf = AB[:, o_q:o_q + 4096]
            kf = AB[:, o_k:o_k + 4096]
            vt = AB[:, o_v:o_v + 4096]
            att = AB[:, o_attnout:o_attnout + 16384]
            dn = AFT[0:16, a_dn:a_dn + 2048]
            vw = AFT[:, a_vw:a_vw + 72]
            cols = AFT[:, a_cols:a_cols + 64]
            dbf = AB[0:64, o_dbf:o_dbf + 1024]
            vg = AB[0:64, o_vg:o_vg + 1024]

            for w in range(NWIN):
                flush_pre(18)
                for pair in range(2):
                    tiles = []
                    for k2 in range(2):
                        sub = pair * 2 + k2
                        par = sub % 2
                        xs = AFT[:, a_xsub + par * 1024: a_xsub + par * 1024 + D]
                        xrg = f"xsub.{par}"
                        P.dma("gpsimd", xs,
                              x_in[s, w * W + sub * 128: w * W + sub * 128 + 128, :],
                              writes=(xrg,))
                        tiles.append((xs, xrg))
                    crgs = (f"cols.{pair}",)
                    ln_stats_batch(tiles, [(AFT[:, a_junk:a_junk + D], "junk")],
                                   cols, crgs, pair * 2, D)
                    for k2 in range(2):
                        sub = pair * 2 + k2
                        i = pair * 2 + k2
                        vop("tensor_scalar",
                            lnwtm[:, sub * 1024:(sub + 1) * 1024], tiles[k2][0],
                            cols[:, 8 + i:8 + i + 1], cols[:, 0 + i:0 + i + 1],
                            ALU.add, ALU.mult,
                            reads=(tiles[k2][1],) + crgs,
                            writes=(f"lnwtm.{sub}",))
                for sub in range(4):
                    for dd in range(8):
                        ti = sub * 8 + dd
                        rt_ = (RT1, RT2)[ti % 2]
                        transp(ps_tt[ti % 2][:, 0:128],
                               lnwtm[:, sub * 1024 + dd * 128: sub * 1024 + dd * 128 + 128],
                               identb[:], reads=(f"lnwtm.{sub}", "identb"),
                               writes=(rt_,))
                        vop("tensor_copy",
                            lnwfm[:, dd * 512 + sub * 128: dd * 512 + sub * 128 + 128],
                            ps_tt[ti % 2][:, 0:128], reads=(rt_,), writes=("lnwfm",))
                for dst, dst_rg, wT, w_rg, bofs in ((qf, "qf", wqT, "wqslot", 0),
                                                    (kf, "kf", wkT, "wkT", D)):
                    for oc in range(8):
                        ps, rg = ((ps_a, R0), (ps_b, R1))[oc % 2]
                        for ic in range(8):
                            mm(ps, wT[:, ic * D + oc * 128: ic * D + oc * 128 + 128],
                               lnwfm[:, ic * 512:(ic + 1) * 512], ic == 0, False,
                               reads=(w_rg, "lnwfm"), writes=(rg,))
                        mm(ps, biasr[:1, bofs + oc * 128: bofs + oc * 128 + 128],
                           onesb[:1, 0:512], False, True,
                           reads=("biasr", "onesb"), writes=(rg,))
                        act(dst[:, oc * 512:(oc + 1) * 512], ps, AF.Identity,
                            reads=(rg,), writes=(dst_rg,))
                for tp4 in range(4):
                    for ic in range(8):
                        for half in range(2):
                            mm((ps_w0, ps_w1)[half],
                               lnwfm[:, ic * 512 + tp4 * 128: ic * 512 + tp4 * 128 + 128],
                               wvT[:, ic * D + half * 512: ic * D + half * 512 + 512],
                               ic == 0, False, reads=("wvT", "lnwfm"),
                               writes=(RW[half],))
                    for half in range(2):
                        mm((ps_w0, ps_w1)[half], onesb[:1, 0:128],
                           biasr[:1, 2 * D + half * 512: 2 * D + half * 512 + 512],
                           False, True, reads=("biasr", "onesb"),
                           writes=(RW[half],))
                    act(vt[:, tp4 * 1024:(tp4 + 1) * 1024], ps_w[:], AF.Identity,
                        reads=RW, writes=("vt",))
                for oc in range(8):
                    ps, rg = ((ps_a, R0), (ps_b, R1))[oc % 2]
                    ps = ps[:, 0:1]
                    for tp4 in range(4):
                        mm(ps, vt[:, tp4 * 1024 + oc * 128: tp4 * 1024 + oc * 128 + 128],
                           ones128[:, 0:1], tp4 == 0, tp4 == 3,
                           reads=("vt", "ones128"), writes=(rg,))
                    vop("tensor_copy", vw[:, w * 8 + oc:w * 8 + oc + 1], ps,
                        reads=(rg,), writes=("vw",))
                for h in range(16):
                    pb, po = 64 * (h % 2), h // 2
                    for sub in range(4):
                        ps, rg = ((ps_a, R0), (ps_b, R1))[sub % 2]
                        mm(ps, kf[pb:pb + 64, po * 512 + sub * 128: po * 512 + sub * 128 + 128],
                           qf[pb:pb + 64, po * 512:(po + 1) * 512], True, True,
                           reads=("kf", "qf"), writes=(rg,))
                        pap = AB[:, o_p + (sub % 2) * 512: o_p + (sub % 2) * 512 + 512]
                        prg = f"p.{sub % 2}"
                        act(pap, ps, AF.Exp, scale=SCALE, reads=(rg,), writes=(prg,))
                        mm(ps_c[0:16, :], selrow[:, h * 16:(h + 1) * 16], pap,
                           (h == 0 and sub == 0), False,
                           reads=("selrow", prg), writes=(R2,))
                        mm(ps_d[0:64, :],
                           vt[:, sub * 1024 + h * 64: sub * 1024 + h * 64 + 64],
                           pap, sub == 0, False, reads=("vt", prg), writes=(R3,))
                    dsl = dbf[:, h * 64 + w * 16: h * 64 + w * 16 + 16]
                    mm(ps_c[0:16, 0:512:GS], selrow[0:64, h * 16:(h + 1) * 16],
                       dsl, False, (h == 15), reads=("selrow", "dbf"), writes=(R2,))
                    mm(ps_d[0:64, 0:512:GS], vg[:, h * 64:(h + 1) * 64], dsl,
                       False, True, reads=("vg", "dbf"), writes=(R3,))
                    act(att[pb:pb + 64, po * 2048 + w * 512: po * 2048 + w * 512 + 512],
                        ps_d[0:64, :], AF.Identity, reads=(R3,),
                        writes=(f"att.{po}",))
                act(dn[:, w * 512:(w + 1) * 512], ps_c[0:16, :], AF.Identity,
                    bias=cpool[0:16, 1:2], reads=(R2, "cpool"), writes=("dn",))

            if a1_hook is not None:
                a1_hook()
            attn_wT(o_wq, 3, "wqslot")

            vtot = vw[:, 32:40]
            vrest = vw[:, 40:72]
            vop("tensor_tensor", vtot, vw[:, 0:8], vw[:, 8:16], ALU.add,
                reads=("vw",), writes=("vw",))
            vop("tensor_tensor", vtot, vtot, vw[:, 16:24], ALU.add,
                reads=("vw",), writes=("vw",))
            vop("tensor_tensor", vtot, vtot, vw[:, 24:32], ALU.add,
                reads=("vw",), writes=("vw",))
            for w in range(NWIN):
                vop("tensor_tensor", vrest[:, w * 8:(w + 1) * 8], vtot,
                    vw[:, w * 8:(w + 1) * 8], ALU.subtract,
                    reads=("vw",), writes=("vw",))
            vop("reciprocal", dn, dn, reads=("dn",), writes=("dn",))

            for w in range(NWIN):
                for oc in range(8):
                    ps, rg = ((ps_a, R0), (ps_b, R1))[(w * 8 + oc) % 2]
                    mm(ps, selbc[0:16, oc * 128:(oc + 1) * 128],
                       dn[:, w * 512:(w + 1) * 512], True, True,
                       reads=("selbc", "dn"), writes=(rg,))
                    sl = att[:, oc * 2048 + w * 512: oc * 2048 + w * 512 + 512]
                    vop("scalar_tensor_tensor", sl, sl,
                        vrest[:, w * 8 + oc:w * 8 + oc + 1], ps, ALU.add, ALU.mult,
                        reads=(f"att.{oc}", "vw", rg), writes=(f"att.{oc}",))

            woT = wqT
            ytiles = [
                (AFT[:, a_xsub:a_xsub + D], "xsub.0"),
                (AFT[:, a_xsub + D:a_xsub + 2 * D], "xsub.1"),
                (AFT[:, a_y:a_y + D], "y.0"),
                (AFT[:, a_y + D:a_y + 2 * D], "y.1"),
            ]
            for g4 in range(4):
                tiles = []
                for k4 in range(4):
                    tt = g4 * 4 + k4
                    yt, yrg = ytiles[k4]
                    for oc in range(8):
                        for half in range(2):
                            mm((ps_w0, ps_w1)[half],
                               att[:, oc * 2048 + tt * 128: oc * 2048 + tt * 128 + 128],
                               woT[:, oc * D + half * 512: oc * D + half * 512 + 512],
                               oc == 0, False, reads=(f"att.{oc}", "wqslot"),
                               writes=(RW[half],))
                    for half in range(2):
                        mm((ps_w0, ps_w1)[half], onesb[:1, 0:128],
                           biasr[:1, 3 * D + half * 512: 3 * D + half * 512 + 512],
                           False, True, reads=("biasr", "onesb"),
                           writes=(RW[half],))
                    act(yt, ps_w[:], AF.Identity, reads=RW, writes=(yrg,))
                    tiles.append((yt, yrg))
                crgs = (f"cols.y{g4 % 2}",)
                i0 = (g4 % 2) * 4
                ln_stats_batch(tiles,
                               [(AB[:, o_lnwtm + i * 1024: o_lnwtm + (i + 1) * 1024],
                                 f"lnwtm.{i}") for i in range(2)],
                               cols, crgs, i0, D, want_nmr=True)
                for k4 in range(4):
                    tt = g4 * 4 + k4
                    par = tt % 2
                    i = i0 + k4
                    xt = AFT[:, a_xt + par * 1024: a_xt + par * 1024 + D]
                    xrg = f"xt.{par}"
                    P.dma("gpsimd", xt, x_in[s, tt * 128:(tt + 1) * 128, :],
                          writes=(xrg,))
                    vop("tensor_scalar", xt, xt, cols[:, 48 + i:48 + i + 1],
                        None, ALU.add, reads=(xrg,) + crgs, writes=(xrg,))
                    vop("scalar_tensor_tensor", xt, ytiles[k4][0],
                        cols[:, 0 + i:0 + i + 1], xt, ALU.mult, ALU.add,
                        reads=(ytiles[k4][1],) + crgs + (xrg,), writes=(xrg,))
                    P.dma("gpsimd",
                          x2d[s * S + tt * 128: s * S + (tt + 1) * 128, :],
                          xt, reads=(xrg,), writes=(f"x2d.{(s * S) // 128 + tt}",))
                flush_pre(8)
            if s == 0:
                attn_wT(o_wq, 0, "wqslot")

        def emit_ffn_block(blk):
            acc = AFT[:, b_acc:b_acc + 8192]
            x2b = AFT[:, b_x2:b_x2 + 4096]
            ln2fm = AB[:, b_ln2fm:b_ln2fm + 8192]
            call = AFT[:, b_call:b_call + 64]
            rawall = AFT[:, b_raw:b_raw + 64]
            gall = AFT[:, b_gall:b_gall + 64]
            tmall = AFT[:, b_tmk:b_tmk + 64]
            rt = AFT[:, b_rt:b_rt + 448]
            hbuf = AB[:, b_h:b_h + 16384]
            bcols = AFT[:, b_cols:b_cols + 64]

            for pair in range(4):
                tiles = []
                for k2 in range(2):
                    ts = pair * 2 + k2
                    x2t = x2b[:, k2 * 1024:(k2 + 1) * 1024]
                    xrg = f"x2t.{k2}"
                    P.dma("gpsimd", x2t,
                          x2d[blk * TB + ts * 128: blk * TB + (ts + 1) * 128, :],
                          reads=(f"x2d.{blk * 8 + ts}",), writes=(xrg,))
                    tiles.append((x2t, xrg))
                i0 = (pair % 2) * 2
                crgs = (f"bcols.{pair % 2}",)
                ln_stats_batch(tiles, [(hbuf[:, i * 1024:(i + 1) * 1024], "hbuf")
                                       for i in range(2)],
                               bcols, crgs, i0, D)
                for k2 in range(2):
                    ts = pair * 2 + k2
                    i = i0 + k2
                    vop("tensor_scalar", acc[:, ts * 1024:(ts + 1) * 1024],
                        tiles[k2][0], bcols[:, 8 + i:8 + i + 1],
                        bcols[:, 0 + i:0 + i + 1], ALU.add, ALU.mult,
                        reads=(tiles[k2][1],) + crgs, writes=(f"acc.{ts}",))
            for ts in range(8):
                par = ts % 2
                l2t = AB[:, b_ln2tm + par * 1024: b_ln2tm + par * 1024 + D]
                lrg = f"l2t.{par}"
                vop("tensor_copy", l2t, acc[:, ts * 1024:(ts + 1) * 1024],
                    reads=(f"acc.{ts}",), writes=(lrg,))
                for dd in range(8):
                    rtp = (RT1, RT2)[dd % 2]
                    transp(ps_tt[dd % 2][:, 0:128], l2t[:, dd * 128:(dd + 1) * 128],
                           identb[:], reads=(lrg, "identb"), writes=(rtp,))
                    vop("tensor_copy",
                        ln2fm[:, dd * 1024 + ts * 128: dd * 1024 + ts * 128 + 128],
                        ps_tt[dd % 2][:, 0:128], reads=(rtp,), writes=("ln2fm",))
                for dd in range(8):
                    pb, rg = tbank[dd % 4]
                    st4 = AFT[:, b_f32st + (dd % 4) * 128:
                              b_f32st + (dd % 4) * 128 + 128]
                    srg = f"f32st.{dd % 4}"
                    transp(pb[:, 0:128],
                           acc[:, ts * 1024 + dd * 128: ts * 1024 + dd * 128 + 128],
                           identf[:], reads=(f"acc.{ts}", "identf"), writes=(rg,))
                    vop("tensor_copy", st4, pb[:, 0:128], reads=(rg,), writes=(srg,))
                    mm(ps_b[:, 0:8], st4, centT[:, dd * 8:(dd + 1) * 8],
                       dd == 0, dd == 7, reads=(srg, "centT"), writes=(R1,))
                vop("tensor_copy", rawall[:, ts * 8:(ts + 1) * 8], ps_b[:, 0:8],
                    reads=(R1,), writes=("rawall",))
            sres = rt[:, 0:64]
            df = rt[:, 64:128]
            s2 = rt[:, 128:192]
            msk1 = rt[:, 192:256]
            msk2 = rt[:, 256:320]
            num = rt[:, 320:384]
            m1 = rt[:, 384:392]
            m2 = rt[:, 392:400]
            den = rt[:, 400:408]
            rden = rt[:, 408:416]
            vop("tensor_tensor", sres, rawall, rbbc8[:], ALU.add,
                reads=("rawall", "rbbc8"), writes=("rt",))
            for ts in range(8):
                vop("reduce_max", m1[:, ts:ts + 1], sres[:, ts * 8:(ts + 1) * 8],
                    AX.X, reads=("rt",), writes=("rt",))
            for ts in range(8):
                vop("tensor_scalar", df[:, ts * 8:(ts + 1) * 8],
                    sres[:, ts * 8:(ts + 1) * 8], m1[:, ts:ts + 1], None,
                    ALU.subtract, reads=("rt",), writes=("rt",))
            vop("tensor_single_scalar", msk1, df, 0.0, ALU.is_ge,
                reads=("rt",), writes=("rt",))
            vop("scalar_tensor_tensor", s2, msk1, -1e30, sres, ALU.mult,
                ALU.add, reads=("rt",), writes=("rt",))
            for ts in range(8):
                vop("reduce_max", m2[:, ts:ts + 1], s2[:, ts * 8:(ts + 1) * 8],
                    AX.X, reads=("rt",), writes=("rt",))
            for ts in range(8):
                vop("tensor_scalar", df[:, ts * 8:(ts + 1) * 8],
                    s2[:, ts * 8:(ts + 1) * 8], m2[:, ts:ts + 1], None,
                    ALU.subtract, reads=("rt",), writes=("rt",))
            vop("tensor_single_scalar", msk2, df, 0.0, ALU.is_ge,
                reads=("rt",), writes=("rt",))
            vop("tensor_tensor", tmall, msk1, msk2, ALU.add,
                reads=("rt",), writes=("tmall",))
            act(gall, rawall, AF.Sigmoid, reads=("rawall",), writes=("gall",))
            act(gall, gall, AF.Exp, reads=("gall",), writes=("gall",))
            vop("tensor_tensor", num, tmall, gall, ALU.mult,
                reads=("tmall", "gall"), writes=("rt",))
            for ts in range(8):
                vop("reduce_sum", den[:, ts:ts + 1], num[:, ts * 8:(ts + 1) * 8],
                    AX.X, reads=("rt",), writes=("rt",))
            vop("reciprocal", rden, den, reads=("rt",), writes=("rt",))
            for ts in range(8):
                vop("tensor_scalar", call[:, ts * 8:(ts + 1) * 8],
                    num[:, ts * 8:(ts + 1) * 8], rden[:, ts:ts + 1], None,
                    ALU.mult, reads=("rt",), writes=("call",))

            wfc_stage = [AB[:, b_wfc + i * 4096: b_wfc + (i + 1) * 4096]
                         for i in range(2)]
            wpbuf = AB[:, b_wp:b_wp + 16384]
            for m in range(NXP):
                for fh in range(2):
                    for q4 in range(4):
                        P.dma("gpsimd",
                              wpbuf[:, q4 * 4096:(q4 + 1) * 4096].rearrange(
                                  "p (j d) -> p j d", j=4),
                              spj[m][fh * 16 + q4 * 4: fh * 16 + (q4 + 1) * 4
                                     ].rearrange("j p d -> p j d"),
                              reads=(f"spj.{m}", "wpT"), writes=("wpT",))

                    def load_jg(jg):
                        par = jg % 2
                        src = sfc[m].rearrange("dd p f -> p dd f")[
                            :, :, fh * 2048 + jg * 512: fh * 2048 + jg * 512 + 512]
                        P.dma("sync",
                              wfc_stage[par].rearrange("p (dd f) -> p dd f", dd=8),
                              src, reads=(f"sfc.{m}", f"wfst.{par}"),
                              writes=(f"wfst.{par}",))
                    load_jg(0)
                    load_jg(1)
                    for j in range(16):
                        jg = j // 4
                        if j % 4 == 0 and 1 <= jg < 3:
                            load_jg(jg + 1)
                        psf, frg0, frg1 = ((ps_f0, R0, R1), (ps_f1, R2, R3))[j % 2]
                        for dd in range(8):
                            for half in range(2):
                                mm(psf[:, half * 512:(half + 1) * 512],
                                   wfc_stage[jg % 2][:, dd * 512 + (j % 4) * 128:
                                                     dd * 512 + (j % 4) * 128 + 128],
                                   ln2fm[:, dd * 1024 + half * 512:
                                         dd * 1024 + half * 512 + 512],
                                   dd == 0, dd == 7,
                                   reads=(f"wfst.{jg % 2}", "ln2fm"),
                                   writes=((frg0, frg1)[half],))
                        act(hbuf[:, j * 1024:(j + 1) * 1024], psf[:], AF.Gelu,
                            reads=(frg0, frg1), writes=("hbuf",))
                    for ts in range(8):
                        for j in range(16):
                            for half in range(2):
                                mm((ps_w0, ps_w1)[half],
                                   hbuf[:, j * 1024 + ts * 128: j * 1024 + ts * 128 + 128],
                                   wpbuf[:, j * 1024 + half * 512: j * 1024 + half * 512 + 512],
                                   j == 0, j == 15,
                                   reads=("hbuf", "wpT"), writes=(RW[half],))
                        asl = acc[:, ts * 1024:(ts + 1) * 1024]
                        if m < 2:
                            vop("tensor_tensor", asl, asl, ps_w[:], ALU.add,
                                reads=(f"acc.{ts}",) + RW, writes=(f"acc.{ts}",))
                        else:
                            vop("scalar_tensor_tensor", asl, ps_w[:],
                                call[:, ts * 8 + (m - 2):ts * 8 + (m - 2) + 1],
                                asl, ALU.mult, ALU.add,
                                reads=(f"acc.{ts}",) + RW + ("call",),
                                writes=(f"acc.{ts}",))

            for g4 in range(2):
                tiles = [(acc[:, (g4 * 4 + k4) * 1024:(g4 * 4 + k4 + 1) * 1024],
                          f"acc.{g4 * 4 + k4}") for k4 in range(4)]
                crgs = ("bcols.y",)
                ln_stats_batch(tiles, [(hbuf[:, i * 1024:(i + 1) * 1024], "hbuf")
                                       for i in range(2)],
                               bcols, crgs, 4, D, want_nmr=True)
                for k4 in range(4):
                    ts = g4 * 4 + k4
                    i = 4 + k4
                    sl3 = 2 + ts % 2
                    xsl = x2b[:, sl3 * 1024:(sl3 + 1) * 1024]
                    xrg = f"x2t.{sl3}"
                    P.dma("gpsimd", xsl,
                          x2d[blk * TB + ts * 128: blk * TB + (ts + 1) * 128, :],
                          reads=(f"x2d.{blk * 8 + ts}",), writes=(xrg,))
                    vop("tensor_scalar", xsl, xsl, bcols[:, 48 + i:48 + i + 1],
                        None, ALU.add, reads=(xrg,) + crgs, writes=(xrg,))
                    vop("scalar_tensor_tensor", xsl, tiles[k4][0],
                        bcols[:, 0 + i:0 + i + 1], xsl, ALU.mult, ALU.add,
                        reads=(tiles[k4][1],) + crgs + (xrg,), writes=(xrg,))
                    P.dma("gpsimd",
                          out_fl[blk * TB + ts * 128: blk * TB + (ts + 1) * 128, :],
                          xsl, reads=(xrg,), writes=(f"outst.{blk}.{ts}",))

        emit_a1(0, None)
        emit_main(0, a1_hook=lambda: emit_a1(1, None))
        emit_main(1)
        while pre_all:
            flush_pre(8)
        P.gate()
        if only_a:
            allx2 = tuple(f"x2d.{i}" for i in range(32))
            P.op("gpsimd", lambda e: e.nop(), reads=allx2)
        else:
            for blk in range(NBLK):
                emit_ffn_block(blk)
            allout = tuple(f"outst.{b}.{t}" for b in range(NBLK) for t in range(8))
            P.op("gpsimd", lambda e: e.nop(), reads=allout)

        P.finalize(nc, block, stk)
    return nc


def build_consts():
    identb = np.eye(128, dtype=ml_dtypes.bfloat16)
    identf = np.eye(128, dtype=np.float32)
    selrow = np.zeros((128, 256), dtype=ml_dtypes.bfloat16)
    for h in range(16):
        selrow[:, 16 * h + h] = 1.0
    selbc = np.zeros((16, 1024), dtype=np.float32)
    for oc in range(8):
        for p in range(128):
            selbc[2 * oc + p // 64, 128 * oc + p] = 1.0
    wmask = np.zeros((64, 64), dtype=ml_dtypes.bfloat16)
    for j in range(64):
        wmask[j, (j // 16) * 16:(j // 16) * 16 + 16] = 1.0
    wmask16 = np.tile(wmask, (1, 16))
    return {"c_identb": identb, "c_identf": identf, "c_selrow": selrow,
            "c_selbc": selbc, "c_wmask16": wmask16}


def kernel(**inputs):
    x = np.ascontiguousarray(np.asarray(inputs["L0_attn_x"], np.float32))
    nc = build()
    base = {
        "wq": np.asarray(inputs["L0_attn_wq"], np.float32),
        "wk": np.asarray(inputs["L0_attn_wk"], np.float32),
        "wv": np.asarray(inputs["L0_attn_wv"], np.float32),
        "wo": np.asarray(inputs["L0_attn_wo"], np.float32),
        "bq": np.asarray(inputs["L0_attn_bq"], np.float32).reshape(1, D),
        "bk": np.asarray(inputs["L0_attn_bk"], np.float32).reshape(1, D),
        "bv": np.asarray(inputs["L0_attn_bv"], np.float32).reshape(1, D),
        "bo": np.asarray(inputs["L0_attn_bo"], np.float32).reshape(1, D),
        "cent": np.asarray(inputs["L0_ffn_centroids"], np.float32),
        "rbias": np.asarray(inputs["L0_ffn_routing_bias"], np.float32).reshape(1, NE),
        "wfc_r": np.asarray(inputs["L0_ffn_Wfc_r"], np.float32),
        "wpj_r": np.asarray(inputs["L0_ffn_Wproj_r"], np.float32),
        "wfc_s": np.asarray(inputs["L0_ffn_Wfc_sh"], np.float32),
        "wpj_s": np.asarray(inputs["L0_ffn_Wproj_sh"], np.float32),
    }
    base.update(build_consts())
    in_maps = []
    for c in range(NC_):
        m = dict(base)
        m["x"] = x[SEQ * c:SEQ * (c + 1)]
        in_maps.append(m)
    res = run_bass_kernel_spmd(nc, in_maps, core_ids=list(range(NC_)))
    out = np.concatenate([np.asarray(r["out"], np.float32) for r in res.results],
                         axis=0)
    return out.reshape(16, S, D)


if __name__ == "__main__":
    nc = build()
    print("instructions:", sum(1 for _ in nc.all_instructions()))


# version 09
# speedup vs baseline: 1.0335x; 1.0335x over previous
import numpy as np
import ml_dtypes
from contextlib import ExitStack

import concourse.bass as bass
import concourse.mybir as mybir
from concourse.bass_utils import run_bass_kernel_spmd

F32 = mybir.dt.float32
BF16 = mybir.dt.bfloat16
AF = mybir.ActivationFunctionType
ALU = mybir.AluOpType
AX = mybir.AxisListType

NC_ = 8
SEQ = 2
S = 2048
D = 1024
W = 512
NWIN = 4
GS = 32
F = 4096
NE = 8
NXP = 10
TB = 1024
NBLK = (SEQ * S) // TB
SCALE = 0.125
EPS = 1e-5
BG = float(S - W)

o_wq, o_wk, o_wv = 0, 8192, 16384
o_attnout = 24576
o_q, o_k = 40960, 45056
o_v = 49152
o_p = 53248
o_lnwfm = 54272
o_lnwtm = 58368
o_lngfm = 62464
o_qg, o_kg = 62976, 63488
o_vg = 64000
o_dbf = 65024
o_stage = 66048
AB_SZ = 70144
b_h = 0
b_wp = 16384
b_ln2fm = 32768
b_wfc = 40960
b_ln2tm = 49152

a_xsub = 0
a_y = 2048
a_xt = 4096
a_junk = 6144
a_dn = 7168
a_xg = 9216
a_gs = 10240
a_cols = 10560
a_vw = 10688
a_cent = 10760
b_acc = 0
b_x2 = 8192
b_f32st = 12288
b_cols = 12800
b_call = 12928
b_raw = 12992
b_gall = 13056
b_tmk = 13120
b_rt = 13184
AF_SZ = 13632

ENGS = ("sync", "gpsimd", "vector", "scalar", "tensor")
QUEUES = ("sync", "gpsimd")
SEM_OF = {"sync": "qS", "gpsimd": "qG", "vector": "tVEC", "scalar": "tACT",
          "tensor": "tMM"}
GATE_EXCLUDE_PFX = ("sfc.", "spj.", "s0f.", "s0p.", "s0a.", "stgF", "stgP",
                    "x2d.", "outst.")
GATE_EXCLUDE = {"identb", "identf", "selrow", "selbc", "wmask16", "onesb",
                "onesf", "ones128", "biasr", "centT", "rbrow", "rbbc8", "cent",
                "cpool"}


class Prog:
    NSLOT = 12

    def __init__(self):
        self.ops = []
        self.res = {}

    def op(self, eng, build, reads=(), writes=(), dma=False, wsmall=False):
        i = len(self.ops)
        deps = set()
        for r in reads:
            st = self.res.get(r)
            if st and st["w"] is not None:
                deps.add(st["w"])
        for r in writes:
            st = self.res.get(r)
            if st:
                if st["w"] is not None:
                    deps.add(st["w"])
                deps.update(st["r"])
        keep = set()
        for j in deps:
            if (self.ops[j][0] == eng and eng not in QUEUES
                    and not self.ops[j][4]
                    and not (eng == "vector" and self.ops[j][5])):
                continue
            keep.add(j)
        for r in reads:
            self.res.setdefault(r, {"w": None, "r": []})["r"].append(i)
        for r in writes:
            self.res[r] = {"w": i, "r": []}
        self.ops.append([eng, build, keep, writes, dma, wsmall])
        return i

    def dma(self, eng, out, in_, reads=(), writes=(), **kw):
        return self.op(eng, lambda e: e.dma_start(out=out, in_=in_, **kw),
                       reads, writes, dma=True)

    def dmat(self, eng, out, in_, reads=(), writes=()):
        return self.op(eng, lambda e: e.dma_start_transpose(out=out, in_=in_),
                       reads, writes, dma=True)

    def gate(self):
        regs = [r for r in self.res
                if r not in GATE_EXCLUDE
                and not any(r.startswith(p) for p in GATE_EXCLUDE_PFX)]
        return self.op("vector", lambda e: e.engine_nop(),
                       reads=(), writes=tuple(regs))

    def finalize(self, nc, block, stk):
        K = Prog.NSLOT
        names = ["tMM", "tACT", "tVEC"]
        names += [f"qS{r}" for r in range(K)] + [f"qG{r}" for r in range(K)]
        sems = {s: stk.enter_context(nc.semaphore(s)) for s in names}
        vpos = {}
        vp = 0
        for i, o in enumerate(self.ops):
            if o[0] == "vector":
                vpos[i] = vp
                vp += 1
        pruned = [set() for _ in self.ops]
        needed = set()
        for i, o in enumerate(self.ops):
            for j in o[2]:
                if (o[0] == "vector" and self.ops[j][0] == "vector"
                        and not self.ops[j][4] and self.ops[j][5]
                        and vpos[i] - vpos[j] >= 8):
                    pruned[i].add(j)
                    continue
                needed.add(j)
        ccnt = {"tMM": 0, "tACT": 0, "tVEC": 0}
        qocc = {"sync": 0, "gpsimd": 0}
        val = [None] * len(self.ops)
        incs = [None] * len(self.ops)
        selfw = [None] * len(self.ops)
        for i, (eng, _, _, _, is_dma, _) in enumerate(self.ops):
            if is_dma:
                qpfx = "qS" if eng == "sync" else "qG"
                occ = qocc[eng]
                qocc[eng] = occ + 1
                slot = f"{qpfx}{occ % K}"
                rnd = occ // K
                if rnd > 0:
                    selfw[i] = (slot, 16 * rnd)
                val[i] = (slot, 16 * (rnd + 1))
                incs[i] = (slot, 16)
            elif i in needed:
                sm = SEM_OF[eng]
                ccnt[sm] += 1
                val[i] = (sm, ccnt[sm])
                incs[i] = (sm, 1)
        streams = {e: [] for e in ENGS}
        for i, (eng, build, deps, _, _, _) in enumerate(self.ops):
            waits = {}
            for j in deps:
                if j in pruned[i] or val[j] is None:
                    continue
                sj, vj = val[j]
                waits[sj] = max(waits.get(sj, 0), vj)
            if selfw[i] is not None:
                sj, vj = selfw[i]
                waits[sj] = max(waits.get(sj, 0), vj)
            streams[eng].append((waits, build, incs[i]))
        for eng in ENGS:
            items = streams[eng]

            def body(e, items=items):
                seen = {}
                for waits, build, inc in items:
                    for s_, v_ in sorted(waits.items()):
                        if seen.get(s_, 0) >= v_:
                            continue
                        e.wait_ge(sems[s_], v_)
                        seen[s_] = v_
                    ins = build(e)
                    if inc is not None:
                        ins.then_inc(sems[inc[0]], inc[1])
            getattr(block, eng)(body)


def build(dbg_x2=False, only_a=False):
    nc = bass.Bass("TRN2", target_bir_lowering=False,
                   detect_race_conditions=False)

    x_in = nc.dram_tensor("x", [SEQ, S, D], F32, kind="ExternalInput").ap()
    wq = nc.dram_tensor("wq", [D, D], F32, kind="ExternalInput").ap()
    wk = nc.dram_tensor("wk", [D, D], F32, kind="ExternalInput").ap()
    wv = nc.dram_tensor("wv", [D, D], F32, kind="ExternalInput").ap()
    wo = nc.dram_tensor("wo", [D, D], F32, kind="ExternalInput").ap()
    bq = nc.dram_tensor("bq", [1, D], F32, kind="ExternalInput").ap()
    bk = nc.dram_tensor("bk", [1, D], F32, kind="ExternalInput").ap()
    bv = nc.dram_tensor("bv", [1, D], F32, kind="ExternalInput").ap()
    bo = nc.dram_tensor("bo", [1, D], F32, kind="ExternalInput").ap()
    cent = nc.dram_tensor("cent", [NE, D], F32, kind="ExternalInput").ap()
    rbias = nc.dram_tensor("rbias", [1, NE], F32, kind="ExternalInput").ap()
    wfc_r = nc.dram_tensor("wfc_r", [NE, F, D], F32, kind="ExternalInput").ap()
    wpj_r = nc.dram_tensor("wpj_r", [NE, D, F], F32, kind="ExternalInput").ap()
    wfc_s = nc.dram_tensor("wfc_s", [2, F, D], F32, kind="ExternalInput").ap()
    wpj_s = nc.dram_tensor("wpj_s", [2, D, F], F32, kind="ExternalInput").ap()
    c_identb = nc.dram_tensor("c_identb", [128, 128], BF16, kind="ExternalInput").ap()
    c_identf = nc.dram_tensor("c_identf", [128, 128], F32, kind="ExternalInput").ap()
    c_selrow = nc.dram_tensor("c_selrow", [128, 256], BF16, kind="ExternalInput").ap()
    c_selbc = nc.dram_tensor("c_selbc", [16, 1024], F32, kind="ExternalInput").ap()
    c_wmask16 = nc.dram_tensor("c_wmask16", [64, 1024], BF16, kind="ExternalInput").ap()

    out_d = nc.dram_tensor("out", [SEQ, S, D], F32, kind="ExternalOutput").ap()
    out_fl = out_d.rearrange("s t d -> (s t) d")
    x2k = "ExternalOutput" if dbg_x2 else "Internal"
    x2d = nc.dram_tensor("x2d", [SEQ * S, D], F32, kind=x2k).ap()
    s0a = nc.dram_tensor("s0a", [4, D, D], BF16, kind="Internal").ap()
    s0f = nc.dram_tensor("s0f", [NXP, F, D], BF16, kind="Internal").ap()
    s0p = nc.dram_tensor("s0p", [NXP, D, F], BF16, kind="Internal").ap()
    sfc = nc.dram_tensor("sfc", [NXP, 8, 128, F], BF16, kind="Internal").ap()
    spj = nc.dram_tensor("spj", [NXP, 32, 128, D], BF16, kind="Internal").ap()

    fc_srcs = [wfc_s[0], wfc_s[1]] + [wfc_r[i] for i in range(NE)]
    pj_srcs = [wpj_s[0], wpj_s[1]] + [wpj_r[i] for i in range(NE)]
    awsrc = {0: wq, 1: wk, 2: wv, 3: wo}

    with ExitStack() as stk:
        e_ = stk.enter_context
        AB = e_(nc.sbuf_tensor("AB", [128, AB_SZ], BF16))
        AFT = e_(nc.sbuf_tensor("AFT", [128, AF_SZ], F32))
        identb = e_(nc.sbuf_tensor("identb", [128, 128], BF16))
        identf = e_(nc.sbuf_tensor("identf", [128, 128], F32))
        selrow = e_(nc.sbuf_tensor("selrow", [128, 256], BF16))
        selbc = e_(nc.sbuf_tensor("selbc", [16, 1024], F32))
        wmask16 = e_(nc.sbuf_tensor("wmask16", [64, 1024], BF16))
        onesb = e_(nc.sbuf_tensor("onesb", [1, 512], BF16))
        onesf = e_(nc.sbuf_tensor("onesf", [1, 128], F32))
        ones128 = e_(nc.sbuf_tensor("ones128", [128, 16], BF16))
        biasr = e_(nc.sbuf_tensor("biasr", [1, 4 * D], BF16))
        centT = e_(nc.sbuf_tensor("centT", [128, 64], F32))
        rbrow = e_(nc.sbuf_tensor("rbrow", [1, 8], F32))
        rbbc8 = e_(nc.sbuf_tensor("rbbc8", [128, 64], F32))
        cpool = e_(nc.sbuf_tensor("cpool", [128, 2], F32))

        ps_f0 = e_(nc.psum_tensor("ps_f0", [128, 1024], F32))
        ps_f1 = e_(nc.psum_tensor("ps_f1", [128, 1024], F32))
        ps_w = e_(nc.psum_tensor("ps_w", [128, 1024], F32))
        ps_t1 = e_(nc.psum_tensor("ps_t1", [128, 1024], BF16))
        ps_t2 = e_(nc.psum_tensor("ps_t2", [128, 1024], BF16))
        ps_a, ps_b = ps_f0[:, 0:512], ps_f0[:, 512:1024]
        ps_c, ps_d = ps_f1[:, 0:512], ps_f1[:, 512:1024]
        ps_w0, ps_w1 = ps_w[:, 0:512], ps_w[:, 512:1024]
        ps_tt = [ps_t1, ps_t2]
        R0, R1, R2, R3 = "ps0", "ps1", "ps2", "ps3"
        RW0, RW1 = "psW0", "psW1"
        RW = (RW0, RW1)
        RT1, RT2 = "psT1", "psT2"
        tbank = None

        P = Prog()
        block = e_(nc.Block())

        def mm(ps_ap, l, r, start, stop, reads=(), writes=()):
            return P.op("tensor",
                        lambda e, o=ps_ap, a=l, b=r, st=start, sp=stop:
                        e.matmul(o, a, b, start=st, stop=sp),
                        reads, writes)

        def transp(ps_ap, in_ap, ident, reads=(), writes=()):
            return P.op("tensor",
                        lambda e, o=ps_ap, i=in_ap, d=ident: e.transpose(o, i, d),
                        reads, writes)

        def act(out_ap, in_ap, fn, bias=0.0, scale=1.0, accum=None,
                reads=(), writes=()):
            return P.op("scalar",
                        lambda e, o=out_ap, i=in_ap, f=fn, b=bias, s=scale, a=accum:
                        e.activation(o, i, f, bias=b, scale=s, accum_out=a),
                        reads, writes)

        def vop(name, *args, reads=(), writes=()):
            small = False
            try:
                shp = args[0].shape
                free = 1
                for d in shp[1:]:
                    free *= d
                small = free <= 64
            except Exception:
                pass
            return P.op("vector",
                        lambda e, n=name, a=args: getattr(e, n)(*a),
                        reads, writes, wsmall=small)

        P.dma("sync", identb[:], c_identb, writes=("identb",))
        P.dma("sync", identf[:], c_identf, writes=("identf",))
        P.dma("sync", selrow[:], c_selrow, writes=("selrow",))
        P.dma("sync", selbc[:], c_selbc, writes=("selbc",))
        P.dma("sync", wmask16[:], c_wmask16, writes=("wmask16",))
        cn = AFT[0:8, a_cent:a_cent + D]
        P.dma("sync", cn, cent, writes=("cent",))
        P.dma("sync", rbrow[:], rbias, writes=("rbrow",))
        vop("memset", onesb[:], 1.0, writes=("onesb",))
        vop("memset", onesf[:], 1.0, writes=("onesf",))
        vop("memset", ones128[:], 1.0, writes=("ones128",))
        vop("memset", cpool[:, 0:1], EPS, writes=("cpool",))
        vop("memset", cpool[:, 1:2], BG, writes=("cpool",))
        P.dma("gpsimd", biasr[:1, 0:D], bq, writes=("biasr",))
        P.dma("gpsimd", biasr[:1, D:2 * D], bk, writes=("biasr",))
        P.dma("gpsimd", biasr[:1, 2 * D:3 * D], bv, writes=("biasr",))
        P.dma("gpsimd", biasr[:1, 3 * D:4 * D], bo, writes=("biasr",))
        for i in range(4):
            P.dma("gpsimd", s0a[i], awsrc[i], writes=(f"s0a.{i}",))

        def attn_wT(slot_off, mi, region):
            for ic in range(8):
                P.dmat("sync", AB[:, slot_off + ic * D: slot_off + (ic + 1) * D],
                       s0a[mi][:, ic * 128:(ic + 1) * 128],
                       reads=(f"s0a.{mi}",), writes=(region,))

        attn_wT(o_wq, 0, "wqslot")
        attn_wT(o_wk, 1, "wkT")
        attn_wT(o_wv, 2, "wvT")

        tbank = [(ps_a, R0), (ps_c, R2), (ps_w0, RW0), (ps_w1, RW1)]
        for dd in range(8):
            pb, rg = tbank[dd % 4]
            transp(pb[:, 0:8], cn[:, dd * 128:(dd + 1) * 128], identf[0:8, 0:8],
                   reads=("cent", "identf"), writes=(rg,))
            vop("tensor_copy", centT[:, dd * 8:(dd + 1) * 8], pb[:, 0:8],
                reads=(rg,), writes=("centT",))
        mm(ps_b[:, 0:8], onesf[:1, 0:128], rbrow[:1, 0:8], True, True,
           reads=("onesf", "rbrow"), writes=(R1,))
        for i in range(8):
            vop("tensor_copy", rbbc8[:, i * 8:(i + 1) * 8], ps_b[:, 0:8],
                reads=(R1,), writes=("rbbc8",))

        pre_all = []

        def mk_ffn_steps(m):
            def cast_fc(h):
                P.dma("gpsimd", s0f[m][h * 2048:(h + 1) * 2048, :],
                      fc_srcs[m][h * 2048:(h + 1) * 2048, :], writes=(f"s0f.{m}",))
            pre_all.append(lambda: cast_fc(0))
            pre_all.append(lambda: cast_fc(1))

            def xst_fc(dd, fh):
                stg = AB[:, o_stage: o_stage + 2048]
                P.dmat("sync", stg,
                       s0f[m][fh * 2048:(fh + 1) * 2048, dd * 128:(dd + 1) * 128],
                       reads=(f"s0f.{m}",), writes=("stgF",))
                P.dma("sync", sfc[m][dd][:, fh * 2048:(fh + 1) * 2048], stg,
                      reads=("stgF",), writes=(f"sfc.{m}",))
            for dd in range(8):
                pre_all.append(lambda dd=dd: (xst_fc(dd, 0), xst_fc(dd, 1)))

            def cast_pj(h):
                P.dma("gpsimd", s0p[m][h * 512:(h + 1) * 512, :],
                      pj_srcs[m][h * 512:(h + 1) * 512, :], writes=(f"s0p.{m}",))
            pre_all.append(lambda: cast_pj(0))
            pre_all.append(lambda: cast_pj(1))

            def xst_pj(j):
                stg = AB[:, o_stage + 2048 + (j % 2) * D:
                         o_stage + 2048 + (j % 2) * D + D]
                rg = f"stgP{j % 2}"
                P.dmat("sync", stg, s0p[m][:, j * 128:(j + 1) * 128],
                       reads=(f"s0p.{m}",), writes=(rg,))
                P.dma("sync", spj[m][j], stg, reads=(rg,), writes=(f"spj.{m}",))
            for j0 in range(0, 32, 4):
                pre_all.append(lambda j0=j0: [xst_pj(j) for j in range(j0, j0 + 4)])

        for m in range(NXP):
            mk_ffn_steps(m)

        def flush_pre(n):
            for _ in range(n):
                if pre_all:
                    pre_all.pop(0)()

        def ln_stats_batch(tiles, junks, colbuf, col_rgs, i0, n, want_nmr=False):
            G = len(tiles)
            for g, (tile, trg) in enumerate(tiles):
                vop("reduce_sum", colbuf[:, 32 + i0 + g:32 + i0 + g + 1], tile,
                    AX.X, reads=(trg,), writes=col_rgs)
            for g, (tile, trg) in enumerate(tiles):
                jk, jrg = junks[g % len(junks)]
                act(jk, tile, AF.Square,
                    accum=colbuf[:, 48 + i0 + g:48 + i0 + g + 1],
                    reads=(trg,), writes=(jrg,) + col_rgs)
            mean = colbuf[:, 64 + i0:64 + i0 + G]
            var = colbuf[:, 80 + i0:80 + i0 + G]
            rstd = colbuf[:, 0 + i0:0 + i0 + G]
            nmean = colbuf[:, 16 + i0:16 + i0 + G]
            ssum = colbuf[:, 32 + i0:32 + i0 + G]
            ssq = colbuf[:, 48 + i0:48 + i0 + G]
            vop("tensor_scalar_mul", mean, ssum, 1.0 / n,
                reads=col_rgs, writes=col_rgs)
            vop("tensor_tensor", var, mean, mean, ALU.mult,
                reads=col_rgs, writes=col_rgs)
            vop("scalar_tensor_tensor", var, ssq, 1.0 / n, var, ALU.mult,
                ALU.subtract, reads=col_rgs, writes=col_rgs)
            act(var, var, AF.Ln, bias=cpool[0:colbuf.shape[0], 0:1],
                reads=col_rgs + ("cpool",), writes=col_rgs)
            act(rstd, var, AF.Exp, scale=-0.5, reads=col_rgs, writes=col_rgs)
            vop("tensor_scalar_mul", nmean, mean, -1.0,
                reads=col_rgs, writes=col_rgs)
            if want_nmr:
                nmr = colbuf[:, 96 + i0:96 + i0 + G]
                vop("tensor_tensor", nmr, nmean, rstd, ALU.mult,
                    reads=col_rgs, writes=col_rgs)

        wqT = AB[:, o_wq:o_wq + 8192]
        wkT = AB[:, o_wk:o_wk + 8192]
        wvT = AB[:, o_wv:o_wv + 8192]

        def emit_a1_stats(s):
            xg = AFT[0:64, a_xg:a_xg + D]
            P.dma("gpsimd", xg, x_in[s, 0:S:GS, :], writes=("xg",))
            gj = AFT[0:64, a_junk:a_junk + D]
            gc = AFT[0:64, a_cols:a_cols + 128]
            ln_stats_batch([(xg, "xg")], [(gj, "junk")], gc, ("cols.g",), 7, D)
            lngtm = AB[0:64, b_ln2tm:b_ln2tm + D]
            vop("tensor_scalar", lngtm, xg, gc[:, 16 + 7:16 + 8], gc[:, 0 + 7:0 + 8],
                ALU.add, ALU.mult, reads=("xg", "cols.g"), writes=("lngtm",))

        def emit_a1_pe(s):
            lngtm = AB[0:64, b_ln2tm:b_ln2tm + D]
            lngf = AB[:, o_lngfm:o_lngfm + 512]
            for dd in range(8):
                rt_ = (RT1, RT2)[dd % 2]
                transp(ps_tt[dd % 2][:, 0:64], lngtm[:, dd * 128:(dd + 1) * 128],
                       identb[0:64, 0:64], reads=("lngtm", "identb"), writes=(rt_,))
                vop("tensor_copy", lngf[:, dd * 64:(dd + 1) * 64],
                    ps_tt[dd % 2][:, 0:64], reads=(rt_,), writes=("lngf",))
            qg = AB[0:64, o_qg:o_qg + 1024]
            kg = AB[0:64, o_p:o_p + 1024]
            vg = AB[0:64, o_vg:o_vg + 1024]
            for dst, dst_rgs, wT, w_rg, bofs in (
                    (qg, ("qg",), wqT, "wqslot", 0),
                    (kg, ("p.0", "p.1"), wkT, "wkT", D)):
                for bank in range(2):
                    psl = (ps_w0, ps_w1)[bank][0:64, :]
                    for hh in range(8):
                        h = bank * 8 + hh
                        for ic in range(8):
                            mm(psl[:, hh * 64:(hh + 1) * 64],
                               wT[:, ic * D + h * 64: ic * D + h * 64 + 64],
                               lngf[:, ic * 64:(ic + 1) * 64],
                               (hh == 0 and ic == 0), False,
                               reads=(w_rg, "lngf"), writes=(RW[bank],))
                        mm(psl[:, hh * 64:(hh + 1) * 64],
                           biasr[:1, bofs + h * 64: bofs + h * 64 + 64],
                           onesb[:1, 0:64],
                           False, (hh == 7),
                           reads=("biasr", "onesb"), writes=(RW[bank],))
                act(dst, ps_w[0:64, :], AF.Identity, reads=RW, writes=dst_rgs)
            for half in range(2):
                psl = (ps_w0, ps_w1)[half][0:64, :]
                for ic in range(8):
                    mm(psl, lngf[:, ic * 64:(ic + 1) * 64],
                       wvT[:, ic * D + half * 512: ic * D + half * 512 + 512],
                       ic == 0, False, reads=("wvT", "lngf"), writes=(RW[half],))
                mm(psl, onesb[:1, 0:64],
                   biasr[:1, 2 * D + half * 512: 2 * D + half * 512 + 512],
                   False, True, reads=("biasr", "onesb"), writes=(RW[half],))
            act(vg, ps_w[0:64, :], AF.Identity, reads=RW, writes=("vg",))
            dbf = AB[0:64, o_dbf:o_dbf + 1024]
            bsg = AFT[0:64, a_xsub:a_xsub + 1024]
            bmg = AFT[0:64, a_y:a_y + 1024]
            for h in range(16):
                pb_ps, pb_rg = ((ps_f0, (R0, R1)), (ps_f1, (R2, R3)))[h // 8]
                hh = h % 8
                mm(pb_ps[0:64, hh * 64:(hh + 1) * 64],
                   kg[:, h * 64:(h + 1) * 64], qg[:, h * 64:(h + 1) * 64],
                   hh == 0, hh == 7,
                   reads=("p.0", "p.1", "qg"), writes=pb_rg)
            vop("tensor_scalar_mul", bsg[:, 0:512], ps_f0[0:64, 0:512], SCALE,
                reads=(R0, R1), writes=("xsub.0",))
            vop("tensor_scalar_mul", bsg[:, 512:1024], ps_f1[0:64, 0:512], SCALE,
                reads=(R2, R3), writes=("xsub.0",))
            vop("tensor_tensor", bmg, bsg, wmask16[:], ALU.mult,
                reads=("xsub.0", "wmask16"), writes=("y.0",))
            vop("tensor_tensor", bsg, bsg, bmg, ALU.add,
                reads=("xsub.0", "y.0"), writes=("xsub.0",))
            act(bsg, bsg, AF.Exp, reads=("xsub.0",), writes=("xsub.0",))
            act(bmg, bmg, AF.Exp, reads=("y.0",), writes=("y.0",))
            vop("tensor_tensor", dbf, bsg, bmg, ALU.subtract,
                reads=("xsub.0", "y.0"), writes=("dbf",))

        def emit_main(s, a1_hook=None, win_hook=None):
            lnwtm = AB[:, o_lnwtm:o_lnwtm + 4096]
            lnwfm = AB[:, o_lnwfm:o_lnwfm + 4096]
            qf = AB[:, o_q:o_q + 4096]
            kf = AB[:, o_k:o_k + 4096]
            vt = AB[:, o_v:o_v + 4096]
            att = AB[:, o_attnout:o_attnout + 16384]
            dn = AFT[0:16, a_dn:a_dn + 2048]
            vw = AFT[:, a_vw:a_vw + 72]
            cols = AFT[:, a_cols:a_cols + 128]
            dbf = AB[0:64, o_dbf:o_dbf + 1024]
            vg = AB[0:64, o_vg:o_vg + 1024]

            for w in range(NWIN):
                flush_pre(18)
                for pair in range(2):
                    tiles = []
                    for k2 in range(2):
                        sub = pair * 2 + k2
                        par = sub % 2
                        xs = AFT[:, a_xsub + par * 1024: a_xsub + par * 1024 + D]
                        xrg = f"xsub.{par}"
                        P.dma("gpsimd", xs,
                              x_in[s, w * W + sub * 128: w * W + sub * 128 + 128, :],
                              writes=(xrg,))
                        tiles.append((xs, xrg))
                    crgs = (f"cols.{pair}",)
                    ln_stats_batch(tiles, [(AFT[:, a_junk:a_junk + D], "junk")],
                                   cols, crgs, pair * 2, D)
                    for k2 in range(2):
                        sub = pair * 2 + k2
                        i = pair * 2 + k2
                        vop("tensor_scalar",
                            lnwtm[:, sub * 1024:(sub + 1) * 1024], tiles[k2][0],
                            cols[:, 16 + i:16 + i + 1], cols[:, 0 + i:0 + i + 1],
                            ALU.add, ALU.mult,
                            reads=(tiles[k2][1],) + crgs,
                            writes=(f"lnwtm.{sub}",))
                for sub in range(4):
                    for dd in range(8):
                        ti = sub * 8 + dd
                        rt_ = (RT1, RT2)[ti % 2]
                        transp(ps_tt[ti % 2][:, 0:128],
                               lnwtm[:, sub * 1024 + dd * 128: sub * 1024 + dd * 128 + 128],
                               identb[:], reads=(f"lnwtm.{sub}", "identb"),
                               writes=(rt_,))
                        vop("tensor_copy",
                            lnwfm[:, dd * 512 + sub * 128: dd * 512 + sub * 128 + 128],
                            ps_tt[ti % 2][:, 0:128], reads=(rt_,), writes=("lnwfm",))
                for dst, dst_rg, wT, w_rg, bofs in ((qf, "qf", wqT, "wqslot", 0),
                                                    (kf, "kf", wkT, "wkT", D)):
                    for oc in range(8):
                        ps, rg = ((ps_a, R0), (ps_b, R1))[oc % 2]
                        for ic in range(8):
                            mm(ps, wT[:, ic * D + oc * 128: ic * D + oc * 128 + 128],
                               lnwfm[:, ic * 512:(ic + 1) * 512], ic == 0, False,
                               reads=(w_rg, "lnwfm"), writes=(rg,))
                        mm(ps, biasr[:1, bofs + oc * 128: bofs + oc * 128 + 128],
                           onesb[:1, 0:512], False, True,
                           reads=("biasr", "onesb"), writes=(rg,))
                        act(dst[:, oc * 512:(oc + 1) * 512], ps, AF.Identity,
                            reads=(rg,), writes=(dst_rg,))
                for tp4 in range(4):
                    for ic in range(8):
                        for half in range(2):
                            mm((ps_w0, ps_w1)[half],
                               lnwfm[:, ic * 512 + tp4 * 128: ic * 512 + tp4 * 128 + 128],
                               wvT[:, ic * D + half * 512: ic * D + half * 512 + 512],
                               ic == 0, False, reads=("wvT", "lnwfm"),
                               writes=(RW[half],))
                    for half in range(2):
                        mm((ps_w0, ps_w1)[half], onesb[:1, 0:128],
                           biasr[:1, 2 * D + half * 512: 2 * D + half * 512 + 512],
                           False, True, reads=("biasr", "onesb"),
                           writes=(RW[half],))
                    act(vt[:, tp4 * 1024:(tp4 + 1) * 1024], ps_w[:], AF.Identity,
                        reads=RW, writes=("vt",))
                for oc in range(8):
                    ps, rg = ((ps_a, R0), (ps_b, R1))[oc % 2]
                    ps = ps[:, 0:1]
                    for tp4 in range(4):
                        mm(ps, vt[:, tp4 * 1024 + oc * 128: tp4 * 1024 + oc * 128 + 128],
                           ones128[:, 0:1], tp4 == 0, tp4 == 3,
                           reads=("vt", "ones128"), writes=(rg,))
                    vop("tensor_copy", vw[:, w * 8 + oc:w * 8 + oc + 1], ps,
                        reads=(rg,), writes=("vw",))
                for h in range(16):
                    pb, po = 64 * (h % 2), h // 2
                    for sub in range(4):
                        ps, rg = ((ps_a, R0), (ps_b, R1))[sub % 2]
                        mm(ps, kf[pb:pb + 64, po * 512 + sub * 128: po * 512 + sub * 128 + 128],
                           qf[pb:pb + 64, po * 512:(po + 1) * 512], True, True,
                           reads=("kf", "qf"), writes=(rg,))
                        pap = AB[:, o_p + (sub % 2) * 512: o_p + (sub % 2) * 512 + 512]
                        prg = f"p.{sub % 2}"
                        act(pap, ps, AF.Exp, scale=SCALE, reads=(rg,), writes=(prg,))
                        mm(ps_c[0:16, :], selrow[:, h * 16:(h + 1) * 16], pap,
                           (h == 0 and sub == 0), False,
                           reads=("selrow", prg), writes=(R2,))
                        mm(ps_d[0:64, :],
                           vt[:, sub * 1024 + h * 64: sub * 1024 + h * 64 + 64],
                           pap, sub == 0, False, reads=("vt", prg), writes=(R3,))
                    dsl = dbf[:, h * 64 + w * 16: h * 64 + w * 16 + 16]
                    mm(ps_c[0:16, 0:512:GS], selrow[0:64, h * 16:(h + 1) * 16],
                       dsl, False, (h == 15), reads=("selrow", "dbf"), writes=(R2,))
                    mm(ps_d[0:64, 0:512:GS], vg[:, h * 64:(h + 1) * 64], dsl,
                       False, True, reads=("vg", "dbf"), writes=(R3,))
                    act(att[pb:pb + 64, po * 2048 + w * 512: po * 2048 + w * 512 + 512],
                        ps_d[0:64, :], AF.Identity, reads=(R3,),
                        writes=(f"att.{po}",))
                act(dn[:, w * 512:(w + 1) * 512], ps_c[0:16, :], AF.Identity,
                    bias=cpool[0:16, 1:2], reads=(R2, "cpool"), writes=("dn",))

            if win_hook is not None:
                win_hook()

            vtot = vw[:, 32:40]
            vrest = vw[:, 40:72]
            vop("tensor_tensor", vtot, vw[:, 0:8], vw[:, 8:16], ALU.add,
                reads=("vw",), writes=("vw",))
            vop("tensor_tensor", vtot, vtot, vw[:, 16:24], ALU.add,
                reads=("vw",), writes=("vw",))
            vop("tensor_tensor", vtot, vtot, vw[:, 24:32], ALU.add,
                reads=("vw",), writes=("vw",))
            for w in range(NWIN):
                vop("tensor_tensor", vrest[:, w * 8:(w + 1) * 8], vtot,
                    vw[:, w * 8:(w + 1) * 8], ALU.subtract,
                    reads=("vw",), writes=("vw",))
            vop("reciprocal", dn, dn, reads=("dn",), writes=("dn",))

            for w in range(NWIN):
                for oc in range(8):
                    ps, rg = ((ps_a, R0), (ps_b, R1))[(w * 8 + oc) % 2]
                    mm(ps, selbc[0:16, oc * 128:(oc + 1) * 128],
                       dn[:, w * 512:(w + 1) * 512], True, True,
                       reads=("selbc", "dn"), writes=(rg,))
                    sl = att[:, oc * 2048 + w * 512: oc * 2048 + w * 512 + 512]
                    vop("scalar_tensor_tensor", sl, sl,
                        vrest[:, w * 8 + oc:w * 8 + oc + 1], ps, ALU.add, ALU.mult,
                        reads=(f"att.{oc}", "vw", rg), writes=(f"att.{oc}",))

            if a1_hook is not None:
                a1_hook()
            attn_wT(o_wq, 3, "wqslot")

            woT = wqT
            ytiles = [
                (AFT[:, a_xsub:a_xsub + D], "xsub.0"),
                (AFT[:, a_xsub + D:a_xsub + 2 * D], "xsub.1"),
                (AFT[:, a_y:a_y + D], "y.0"),
                (AFT[:, a_y + D:a_y + 2 * D], "y.1"),
            ]
            for g4 in range(4):
                tiles = []
                for k4 in range(4):
                    tt = g4 * 4 + k4
                    yt, yrg = ytiles[k4]
                    for oc in range(8):
                        for half in range(2):
                            mm((ps_w0, ps_w1)[half],
                               att[:, oc * 2048 + tt * 128: oc * 2048 + tt * 128 + 128],
                               woT[:, oc * D + half * 512: oc * D + half * 512 + 512],
                               oc == 0, False, reads=(f"att.{oc}", "wqslot"),
                               writes=(RW[half],))
                    for half in range(2):
                        mm((ps_w0, ps_w1)[half], onesb[:1, 0:128],
                           biasr[:1, 3 * D + half * 512: 3 * D + half * 512 + 512],
                           False, True, reads=("biasr", "onesb"),
                           writes=(RW[half],))
                    act(yt, ps_w[:], AF.Identity, reads=RW, writes=(yrg,))
                    tiles.append((yt, yrg))
                crgs = (f"cols.y{g4 % 2}",)
                i0 = (g4 % 2) * 4
                ln_stats_batch(tiles,
                               [(AB[:, o_lnwtm + i * 1024: o_lnwtm + (i + 1) * 1024],
                                 f"lnwtm.{i}") for i in range(2)],
                               cols, crgs, i0, D, want_nmr=True)
                for k4 in range(4):
                    tt = g4 * 4 + k4
                    par = tt % 2
                    i = i0 + k4
                    xt = AFT[:, a_xt + par * 1024: a_xt + par * 1024 + D]
                    xrg = f"xt.{par}"
                    P.dma("gpsimd", xt, x_in[s, tt * 128:(tt + 1) * 128, :],
                          writes=(xrg,))
                    vop("tensor_scalar", xt, xt, cols[:, 96 + i:96 + i + 1],
                        None, ALU.add, reads=(xrg,) + crgs, writes=(xrg,))
                    vop("scalar_tensor_tensor", xt, ytiles[k4][0],
                        cols[:, 0 + i:0 + i + 1], xt, ALU.mult, ALU.add,
                        reads=(ytiles[k4][1],) + crgs + (xrg,), writes=(xrg,))
                    P.dma("gpsimd",
                          x2d[s * S + tt * 128: s * S + (tt + 1) * 128, :],
                          xt, reads=(xrg,), writes=(f"x2d.{(s * S) // 128 + tt}",))
                flush_pre(8)
            if s == 0:
                attn_wT(o_wq, 0, "wqslot")

        def emit_b1_stats(blk):
            acc = AFT[:, b_acc:b_acc + 8192]
            x2b = AFT[:, b_x2:b_x2 + 4096]
            ln2fm = AB[:, b_ln2fm:b_ln2fm + 8192]
            call = AFT[:, b_call:b_call + 64]
            rawall = AFT[:, b_raw:b_raw + 64]
            gall = AFT[:, b_gall:b_gall + 64]
            tmall = AFT[:, b_tmk:b_tmk + 64]
            rt = AFT[:, b_rt:b_rt + 448]
            hbuf = AB[:, b_h:b_h + 16384]
            bcols = AFT[:, b_cols:b_cols + 128]

            for pair in range(4):
                tiles = []
                for k2 in range(2):
                    ts = pair * 2 + k2
                    x2t = x2b[:, k2 * 1024:(k2 + 1) * 1024]
                    xrg = f"x2t.{k2}"
                    P.dma("gpsimd", x2t,
                          x2d[blk * TB + ts * 128: blk * TB + (ts + 1) * 128, :],
                          reads=(f"x2d.{blk * 8 + ts}",), writes=(xrg,))
                    tiles.append((x2t, xrg))
                crgs = (f"bcols.{pair}",)
                ln_stats_batch(tiles, [(hbuf[:, i * 1024:(i + 1) * 1024], "hbuf")
                                       for i in range(2)],
                               bcols, crgs, pair * 2, D)

        def emit_b1_rest(blk):
            acc = AFT[:, b_acc:b_acc + 8192]
            x2b = AFT[:, b_x2:b_x2 + 4096]
            ln2fm = AB[:, b_ln2fm:b_ln2fm + 8192]
            call = AFT[:, b_call:b_call + 64]
            rawall = AFT[:, b_raw:b_raw + 64]
            gall = AFT[:, b_gall:b_gall + 64]
            tmall = AFT[:, b_tmk:b_tmk + 64]
            rt = AFT[:, b_rt:b_rt + 448]
            hbuf = AB[:, b_h:b_h + 16384]
            bcols = AFT[:, b_cols:b_cols + 128]

            for ts in range(8):
                k2 = ts % 2
                x2t = x2b[:, k2 * 1024:(k2 + 1) * 1024]
                xrg = f"x2t.{k2}"
                P.dma("gpsimd", x2t,
                      x2d[blk * TB + ts * 128: blk * TB + (ts + 1) * 128, :],
                      reads=(f"x2d.{blk * 8 + ts}",), writes=(xrg,))
                vop("tensor_scalar", acc[:, ts * 1024:(ts + 1) * 1024],
                    x2t, bcols[:, 16 + ts:16 + ts + 1],
                    bcols[:, 0 + ts:0 + ts + 1], ALU.add, ALU.mult,
                    reads=(xrg, f"bcols.{ts // 2}"), writes=(f"acc.{ts}",))
            for ts in range(8):
                par = ts % 2
                l2t = AB[:, b_ln2tm + par * 1024: b_ln2tm + par * 1024 + D]
                lrg = f"l2t.{par}"
                vop("tensor_copy", l2t, acc[:, ts * 1024:(ts + 1) * 1024],
                    reads=(f"acc.{ts}",), writes=(lrg,))
                for dd in range(8):
                    rtp = (RT1, RT2)[dd % 2]
                    transp(ps_tt[dd % 2][:, 0:128], l2t[:, dd * 128:(dd + 1) * 128],
                           identb[:], reads=(lrg, "identb"), writes=(rtp,))
                    vop("tensor_copy",
                        ln2fm[:, dd * 1024 + ts * 128: dd * 1024 + ts * 128 + 128],
                        ps_tt[dd % 2][:, 0:128], reads=(rtp,), writes=("ln2fm",))
                for dd in range(8):
                    pb, rg = tbank[dd % 4]
                    st4 = AFT[:, b_f32st + (dd % 4) * 128:
                              b_f32st + (dd % 4) * 128 + 128]
                    srg = f"f32st.{dd % 4}"
                    transp(pb[:, 0:128],
                           acc[:, ts * 1024 + dd * 128: ts * 1024 + dd * 128 + 128],
                           identf[:], reads=(f"acc.{ts}", "identf"), writes=(rg,))
                    vop("tensor_copy", st4, pb[:, 0:128], reads=(rg,), writes=(srg,))
                    mm(ps_b[:, 0:8], st4, centT[:, dd * 8:(dd + 1) * 8],
                       dd == 0, dd == 7, reads=(srg, "centT"), writes=(R1,))
                vop("tensor_copy", rawall[:, ts * 8:(ts + 1) * 8], ps_b[:, 0:8],
                    reads=(R1,), writes=("rawall",))
            sres = rt[:, 0:64]
            df = rt[:, 64:128]
            s2 = rt[:, 128:192]
            msk1 = rt[:, 192:256]
            msk2 = rt[:, 256:320]
            num = rt[:, 320:384]
            m1 = rt[:, 384:392]
            m2 = rt[:, 392:400]
            den = rt[:, 400:408]
            rden = rt[:, 408:416]
            vop("tensor_tensor", sres, rawall, rbbc8[:], ALU.add,
                reads=("rawall", "rbbc8"), writes=("rt",))
            for ts in range(8):
                vop("reduce_max", m1[:, ts:ts + 1], sres[:, ts * 8:(ts + 1) * 8],
                    AX.X, reads=("rt",), writes=("rt",))
            for ts in range(8):
                vop("tensor_scalar", df[:, ts * 8:(ts + 1) * 8],
                    sres[:, ts * 8:(ts + 1) * 8], m1[:, ts:ts + 1], None,
                    ALU.subtract, reads=("rt",), writes=("rt",))
            vop("tensor_single_scalar", msk1, df, 0.0, ALU.is_ge,
                reads=("rt",), writes=("rt",))
            vop("scalar_tensor_tensor", s2, msk1, -1e30, sres, ALU.mult,
                ALU.add, reads=("rt",), writes=("rt",))
            for ts in range(8):
                vop("reduce_max", m2[:, ts:ts + 1], s2[:, ts * 8:(ts + 1) * 8],
                    AX.X, reads=("rt",), writes=("rt",))
            for ts in range(8):
                vop("tensor_scalar", df[:, ts * 8:(ts + 1) * 8],
                    s2[:, ts * 8:(ts + 1) * 8], m2[:, ts:ts + 1], None,
                    ALU.subtract, reads=("rt",), writes=("rt",))
            vop("tensor_single_scalar", msk2, df, 0.0, ALU.is_ge,
                reads=("rt",), writes=("rt",))
            vop("tensor_tensor", tmall, msk1, msk2, ALU.add,
                reads=("rt",), writes=("tmall",))
            act(gall, rawall, AF.Sigmoid, reads=("rawall",), writes=("gall",))
            act(gall, gall, AF.Exp, reads=("gall",), writes=("gall",))
            vop("tensor_tensor", num, tmall, gall, ALU.mult,
                reads=("tmall", "gall"), writes=("rt",))
            for ts in range(8):
                vop("reduce_sum", den[:, ts:ts + 1], num[:, ts * 8:(ts + 1) * 8],
                    AX.X, reads=("rt",), writes=("rt",))
            vop("reciprocal", rden, den, reads=("rt",), writes=("rt",))
            for ts in range(8):
                vop("tensor_scalar", call[:, ts * 8:(ts + 1) * 8],
                    num[:, ts * 8:(ts + 1) * 8], rden[:, ts:ts + 1], None,
                    ALU.mult, reads=("rt",), writes=("call",))


        def emit_b2(blk):
            acc = AFT[:, b_acc:b_acc + 8192]
            x2b = AFT[:, b_x2:b_x2 + 4096]
            ln2fm = AB[:, b_ln2fm:b_ln2fm + 8192]
            call = AFT[:, b_call:b_call + 64]
            rawall = AFT[:, b_raw:b_raw + 64]
            gall = AFT[:, b_gall:b_gall + 64]
            tmall = AFT[:, b_tmk:b_tmk + 64]
            rt = AFT[:, b_rt:b_rt + 448]
            hbuf = AB[:, b_h:b_h + 16384]
            bcols = AFT[:, b_cols:b_cols + 128]

            wfc_stage = [AB[:, b_wfc + i * 4096: b_wfc + (i + 1) * 4096]
                         for i in range(2)]
            wpbuf = AB[:, b_wp:b_wp + 16384]
            for m in range(NXP):
                for fh in range(2):
                    for q4 in range(4):
                        P.dma("gpsimd",
                              wpbuf[:, q4 * 4096:(q4 + 1) * 4096].rearrange(
                                  "p (j d) -> p j d", j=4),
                              spj[m][fh * 16 + q4 * 4: fh * 16 + (q4 + 1) * 4
                                     ].rearrange("j p d -> p j d"),
                              reads=(f"spj.{m}", "wpT"), writes=("wpT",))

                    def load_jg(jg):
                        par = jg % 2
                        src = sfc[m].rearrange("dd p f -> p dd f")[
                            :, :, fh * 2048 + jg * 512: fh * 2048 + jg * 512 + 512]
                        P.dma("sync",
                              wfc_stage[par].rearrange("p (dd f) -> p dd f", dd=8),
                              src, reads=(f"sfc.{m}", f"wfst.{par}"),
                              writes=(f"wfst.{par}",))
                    load_jg(0)
                    load_jg(1)
                    for j in range(16):
                        jg = j // 4
                        if j % 4 == 0 and 1 <= jg < 3:
                            load_jg(jg + 1)
                        psf, frg0, frg1 = ((ps_f0, R0, R1), (ps_f1, R2, R3))[j % 2]
                        for dd in range(8):
                            for half in range(2):
                                mm(psf[:, half * 512:(half + 1) * 512],
                                   wfc_stage[jg % 2][:, dd * 512 + (j % 4) * 128:
                                                     dd * 512 + (j % 4) * 128 + 128],
                                   ln2fm[:, dd * 1024 + half * 512:
                                         dd * 1024 + half * 512 + 512],
                                   dd == 0, dd == 7,
                                   reads=(f"wfst.{jg % 2}", "ln2fm"),
                                   writes=((frg0, frg1)[half],))
                        act(hbuf[:, j * 1024:(j + 1) * 1024], psf[:], AF.Gelu,
                            reads=(frg0, frg1), writes=("hbuf",))
                    for ts in range(8):
                        for j in range(16):
                            for half in range(2):
                                mm((ps_w0, ps_w1)[half],
                                   hbuf[:, j * 1024 + ts * 128: j * 1024 + ts * 128 + 128],
                                   wpbuf[:, j * 1024 + half * 512: j * 1024 + half * 512 + 512],
                                   j == 0, j == 15,
                                   reads=("hbuf", "wpT"), writes=(RW[half],))
                        asl = acc[:, ts * 1024:(ts + 1) * 1024]
                        if m < 2:
                            vop("tensor_tensor", asl, asl, ps_w[:], ALU.add,
                                reads=(f"acc.{ts}",) + RW, writes=(f"acc.{ts}",))
                        else:
                            vop("scalar_tensor_tensor", asl, ps_w[:],
                                call[:, ts * 8 + (m - 2):ts * 8 + (m - 2) + 1],
                                asl, ALU.mult, ALU.add,
                                reads=(f"acc.{ts}",) + RW + ("call",),
                                writes=(f"acc.{ts}",))


        def emit_b3(blk):
            acc = AFT[:, b_acc:b_acc + 8192]
            x2b = AFT[:, b_x2:b_x2 + 4096]
            ln2fm = AB[:, b_ln2fm:b_ln2fm + 8192]
            call = AFT[:, b_call:b_call + 64]
            rawall = AFT[:, b_raw:b_raw + 64]
            gall = AFT[:, b_gall:b_gall + 64]
            tmall = AFT[:, b_tmk:b_tmk + 64]
            rt = AFT[:, b_rt:b_rt + 448]
            hbuf = AB[:, b_h:b_h + 16384]
            bcols = AFT[:, b_cols:b_cols + 128]

            for g4 in range(2):
                tiles = [(acc[:, (g4 * 4 + k4) * 1024:(g4 * 4 + k4 + 1) * 1024],
                          f"acc.{g4 * 4 + k4}") for k4 in range(4)]
                crgs = ("bcols.y",)
                ln_stats_batch(tiles, [(hbuf[:, i * 1024:(i + 1) * 1024], "hbuf")
                                       for i in range(2)],
                               bcols, crgs, 8 + g4 * 4, D, want_nmr=True)
                for k4 in range(4):
                    ts = g4 * 4 + k4
                    i = 8 + g4 * 4 + k4
                    sl3 = 2 + ts % 2
                    xsl = x2b[:, sl3 * 1024:(sl3 + 1) * 1024]
                    xrg = f"x2t.{sl3}"
                    P.dma("gpsimd", xsl,
                          x2d[blk * TB + ts * 128: blk * TB + (ts + 1) * 128, :],
                          reads=(f"x2d.{blk * 8 + ts}",), writes=(xrg,))
                    vop("tensor_scalar", xsl, xsl, bcols[:, 96 + i:96 + i + 1],
                        None, ALU.add, reads=(xrg,) + crgs, writes=(xrg,))
                    vop("scalar_tensor_tensor", xsl, tiles[k4][0],
                        bcols[:, 0 + i:0 + i + 1], xsl, ALU.mult, ALU.add,
                        reads=(tiles[k4][1],) + crgs + (xrg,), writes=(xrg,))
                    P.dma("gpsimd",
                          out_fl[blk * TB + ts * 128: blk * TB + (ts + 1) * 128, :],
                          xsl, reads=(xrg,), writes=(f"outst.{blk}.{ts}",))


        emit_a1_stats(0)
        emit_a1_pe(0)

        def hook1():
            emit_a1_pe(1)
        emit_a1_stats_done = [False]

        def win_hook():
            if not emit_a1_stats_done[0]:
                emit_a1_stats(1)
                emit_a1_stats_done[0] = True
        emit_main(0, a1_hook=hook1, win_hook=win_hook)
        emit_main(1)
        while pre_all:
            flush_pre(8)
        P.gate()
        if only_a:
            allx2 = tuple(f"x2d.{i}" for i in range(32))
            P.op("gpsimd", lambda e: e.nop(), reads=allx2)
        else:
            emit_b1_stats(0)
            emit_b1_rest(0)
            for blk in range(NBLK):
                emit_b2(blk)
                if blk + 1 < NBLK:
                    emit_b1_stats(blk + 1)
                emit_b3(blk)
                if blk + 1 < NBLK:
                    emit_b1_rest(blk + 1)
            allout = tuple(f"outst.{b}.{t}" for b in range(NBLK) for t in range(8))
            P.op("gpsimd", lambda e: e.nop(), reads=allout)

        P.finalize(nc, block, stk)
    return nc


def build_consts():
    identb = np.eye(128, dtype=ml_dtypes.bfloat16)
    identf = np.eye(128, dtype=np.float32)
    selrow = np.zeros((128, 256), dtype=ml_dtypes.bfloat16)
    for h in range(16):
        selrow[:, 16 * h + h] = 1.0
    selbc = np.zeros((16, 1024), dtype=np.float32)
    for oc in range(8):
        for p in range(128):
            selbc[2 * oc + p // 64, 128 * oc + p] = 1.0
    wmask = np.zeros((64, 64), dtype=ml_dtypes.bfloat16)
    for j in range(64):
        wmask[j, (j // 16) * 16:(j // 16) * 16 + 16] = 1.0
    wmask16 = np.tile(wmask, (1, 16))
    return {"c_identb": identb, "c_identf": identf, "c_selrow": selrow,
            "c_selbc": selbc, "c_wmask16": wmask16}


def kernel(**inputs):
    x = np.ascontiguousarray(np.asarray(inputs["L0_attn_x"], np.float32))
    nc = build()
    base = {
        "wq": np.asarray(inputs["L0_attn_wq"], np.float32),
        "wk": np.asarray(inputs["L0_attn_wk"], np.float32),
        "wv": np.asarray(inputs["L0_attn_wv"], np.float32),
        "wo": np.asarray(inputs["L0_attn_wo"], np.float32),
        "bq": np.asarray(inputs["L0_attn_bq"], np.float32).reshape(1, D),
        "bk": np.asarray(inputs["L0_attn_bk"], np.float32).reshape(1, D),
        "bv": np.asarray(inputs["L0_attn_bv"], np.float32).reshape(1, D),
        "bo": np.asarray(inputs["L0_attn_bo"], np.float32).reshape(1, D),
        "cent": np.asarray(inputs["L0_ffn_centroids"], np.float32),
        "rbias": np.asarray(inputs["L0_ffn_routing_bias"], np.float32).reshape(1, NE),
        "wfc_r": np.asarray(inputs["L0_ffn_Wfc_r"], np.float32),
        "wpj_r": np.asarray(inputs["L0_ffn_Wproj_r"], np.float32),
        "wfc_s": np.asarray(inputs["L0_ffn_Wfc_sh"], np.float32),
        "wpj_s": np.asarray(inputs["L0_ffn_Wproj_sh"], np.float32),
    }
    base.update(build_consts())
    in_maps = []
    for c in range(NC_):
        m = dict(base)
        m["x"] = x[SEQ * c:SEQ * (c + 1)]
        in_maps.append(m)
    res = run_bass_kernel_spmd(nc, in_maps, core_ids=list(range(NC_)))
    out = np.concatenate([np.asarray(r["out"], np.float32) for r in res.results],
                         axis=0)
    return out.reshape(16, S, D)


if __name__ == "__main__":
    nc = build()
    print("instructions:", sum(1 for _ in nc.all_instructions()))
